# version 10 (final)
# baseline (speedup 1.0000x reference)
import contextlib
import numpy as np
import ml_dtypes
import concourse.bass as bass
import concourse.mybir as mybir
from concourse.bass_utils import run_bass_kernel_spmd

F32 = mybir.dt.float32
BF16 = mybir.dt.bfloat16
I32 = mybir.dt.int32
AF = mybir.ActivationFunctionType
ALU = mybir.AluOpType
AX = mybir.AxisListType

T, D, H, HD, HID = 4096, 1024, 16, 64, 3072
NCH = T // 128
NQT = T // 512
QK_SCALE = float(HD) ** -0.25
FFN_S = float(np.float32(1.0) / np.float32(127.0))
FFN_S2 = float(np.float32(FFN_S) * np.float32(FFN_S))
INV_D = 1.0 / D
LN_EPS = 1e-5

_CONSTS = None


def _host_consts():
    global _CONSTS
    if _CONSTS is not None:
        return _CONSTS
    import jax
    import jax.numpy as jnp
    with jax.default_device(jax.devices("cpu")[0]):
        theta = jnp.exp(jnp.log(10000.0) * jnp.arange(HD // 2, dtype=jnp.float32) / HD)
        args = jnp.arange(T, dtype=jnp.float32)[:, None] * theta
        c = np.asarray(jnp.cos(args), dtype=np.float32)
        s = np.asarray(jnp.sin(args), dtype=np.float32)
    C64 = np.empty((T, HD), np.float32)
    C64[:, 0::2] = c
    C64[:, 1::2] = c
    Ss64 = np.empty((T, HD), np.float32)
    Ss64[:, 0::2] = -s
    Ss64[:, 1::2] = s
    tabC = np.tile(C64 * QK_SCALE, (1, H)).astype(ml_dtypes.bfloat16)
    tabS = np.tile(Ss64 * QK_SCALE, (1, H)).astype(ml_dtypes.bfloat16)
    pidx = np.arange(128)[:, None]
    fidx = np.arange(512)[None, :]
    masks = np.stack([(fidx >= (128 * r + pidx)) for r in range(4)]).astype(np.float32)
    masks = masks.astype(ml_dtypes.bfloat16)
    _CONSTS = dict(
        tabC=tabC, tabS=tabS, masks=masks,
        identb=np.eye(128, dtype=ml_dtypes.bfloat16),
        identf=np.eye(128, dtype=np.float32),
        onesb=np.ones((1, 64), dtype=ml_dtypes.bfloat16),
        onesf=np.ones((1, 128), dtype=np.float32),
    )
    return _CONSTS


class Plan:

    def __init__(self, sems):
        self.ops = []
        self.cnt = {}
        self.sems = sems
        self.lo = 0

    def tok(self, sem):
        return self.cnt.get(sem, 0)

    PARANOID = False
    SELFSYNC = True

    def op(self, eng, fn, waits=(), incs=(), selfsync=True):
        if Plan.PARANOID:
            waits = list(waits) + [(s, self.tok(s)) for s in self.cnt if s not in [i[0] for i in incs]]
        if Plan.SELFSYNC and selfsync and eng in ("vector", "scalar"):
            sem = "dve" if eng == "vector" else "act"
            waits = list(waits) + [(sem, self.tok(sem))]
            if not any(s == sem for s, _ in incs):
                incs = list(incs) + [(sem, 1)]
        waits = [(s, v) for (s, v) in waits if v > 0]
        for s, amt in incs:
            self.cnt[s] = self.cnt.get(s, 0) + amt
        self.ops.append((eng, fn, waits, list(incs)))
        return {s: self.cnt[s] for s, _ in incs}

    def barrier(self):
        base = self.tok("ph")
        for e in ("sync", "gpsimd", "vector", "scalar", "tensor"):
            self.op(e, lambda eng: eng.nop(), incs=[("ph", 1)], selfsync=False)
        for e in ("sync", "gpsimd", "vector", "scalar", "tensor"):
            self.op(e, lambda eng: eng.nop(), waits=[("ph", base + 5)], selfsync=False)

    def emit_phase(self, nc):
        ops = self.ops[self.lo:]
        self.lo = len(self.ops)
        with nc.Block() as block:
            def section(eng_name):
                def run(eng):
                    for e, fn, waits, incs in ops:
                        if e != eng_name:
                            continue
                        for s, v in waits:
                            eng.wait_ge(self.sems[s], v)
                        instr = fn(eng)
                        for s, amt in incs:
                            instr.then_inc(self.sems[s], amt)
                return run
            block.sync(section("sync"))
            block.gpsimd(section("gpsimd"))
            block.vector(section("vector"))
            block.scalar(section("scalar"))
            block.tensor(section("tensor"))


SEM_NAMES = (
    "pe", "dve", "act", "pool", "ph",
    "lx0", "lx1", "lt0", "lt1", "spx0", "spx1", "lw",
    "lxn", "lq0", "lq1", "spo0", "spo1",
    "l3a0", "l3a1", "l3b0", "l3b1", "out0", "out1",
)


def build_nc(dbg=()):
    nc = bass.Bass("TRN2", target_bir_lowering=False, debug=False,
                   detect_race_conditions=False)

    x_h = nc.dram_tensor("x", [T, D], F32, kind="ExternalInput").ap()
    wq_h = nc.dram_tensor("wq", [D, D], F32, kind="ExternalInput").ap()
    wk_h = nc.dram_tensor("wk", [D, D], F32, kind="ExternalInput").ap()
    wv_h = nc.dram_tensor("wv", [D, D], F32, kind="ExternalInput").ap()
    wo_h = nc.dram_tensor("wo", [D, D], F32, kind="ExternalInput").ap()
    wg_h = nc.dram_tensor("wg", [HID, D], I32, kind="ExternalInput").ap()
    wu_h = nc.dram_tensor("wu", [HID, D], I32, kind="ExternalInput").ap()
    wd_h = nc.dram_tensor("wd", [D, HID], I32, kind="ExternalInput").ap()
    tabC_h = nc.dram_tensor("tabC", [T, D], BF16, kind="ExternalInput").ap()
    tabS_h = nc.dram_tensor("tabS", [T, D], BF16, kind="ExternalInput").ap()
    masks_h = nc.dram_tensor("masks", [4, 128, 512], BF16, kind="ExternalInput").ap()
    identb_h = nc.dram_tensor("identb_h", [128, 128], BF16, kind="ExternalInput").ap()
    identf_h = nc.dram_tensor("identf_h", [128, 128], F32, kind="ExternalInput").ap()
    onesb_h = nc.dram_tensor("onesb_h", [1, 64], BF16, kind="ExternalInput").ap()
    onesf_h = nc.dram_tensor("onesf_h", [1, 128], F32, kind="ExternalInput").ap()
    out_h = nc.dram_tensor("out", [T, D], F32, kind="ExternalOutput").ap()

    xn_kind = "ExternalOutput"
    o_kind = "ExternalOutput"
    xn_sp = nc.dram_tensor("xn_sp", [NCH, 128, 8, 128], BF16, kind=xn_kind).ap()
    o_sp = nc.dram_tensor("o_sp", [NCH, 128, 8, 128], BF16, kind=o_kind).ap()
    if "kv" in dbg:
        k_dbg = nc.dram_tensor("k_dbg", [128, 8, T], BF16, kind="ExternalOutput").ap()
        v_dbg = nc.dram_tensor("v_dbg", [128, NCH, 1040], BF16, kind="ExternalOutput").ap()
    if "wt" in dbg:
        wkT_dbg = nc.dram_tensor("wkT_dbg", [128, 8, D], BF16, kind="ExternalOutput").ap()
        wvT_dbg = nc.dram_tensor("wvT_dbg", [128, 8, D], BF16, kind="ExternalOutput").ap()

    with contextlib.ExitStack() as st:
        sems = {n: st.enter_context(nc.semaphore(n)) for n in SEM_NAMES}
        p = Plan(sems)

        identb = st.enter_context(nc.sbuf_tensor("identb", [128, 128], BF16))
        identf = st.enter_context(nc.sbuf_tensor("identf", [128, 128], F32))
        onesb = st.enter_context(nc.sbuf_tensor("onesb", [1, 64], BF16))
        onesf = st.enter_context(nc.sbuf_tensor("onesf", [1, 128], F32))
        epscol = st.enter_context(nc.sbuf_tensor("epscol", [128, 1], F32))
        Mkcol = st.enter_context(nc.sbuf_tensor("Mkcol", [16, 1], F32))

        def prep_weight(w_src, rows, wT, nblk, stage, pwps):
            seg = stage.shape[1]
            for s0 in range(0, rows, seg):
                ns = min(seg, rows - s0)
                p.op("gpsimd",
                     lambda g, s0=s0, ns=ns: g.dma_start(out=stage[:, 0:ns, :], in_=w_src[:, s0:s0 + ns, :]),
                     waits=[("pe", p.tok("pe")), ("lw", p.tok("lw"))], incs=[("lw", 16)])
                lwtok = p.tok("lw")
                for blk in range(nblk):
                    for gl in range(0, ns, 8):
                        ng = min(8, ns - gl)
                        for al in range(gl, gl + ng):
                            p.op("tensor",
                                 lambda t_, al=al, blk=blk, gl=gl: t_.transpose(
                                     pwps[:, (al - gl) * 128:(al - gl + 1) * 128],
                                     stage[:, al, blk * 128:(blk + 1) * 128], identb[:]),
                                 waits=([("lw", lwtok), ("dve", p.tok("dve"))] if al == gl else []),
                                 incs=([("pe", 1)] if al == gl + ng - 1 else []))
                        p.op("vector",
                             lambda v, blk=blk, s0=s0, gl=gl, ng=ng: v.tensor_copy(
                                 wT[:, blk, (s0 + gl) * 128:(s0 + gl + ng) * 128],
                                 pwps[:, 0:ng * 128]),
                             waits=[("pe", p.tok("pe"))], incs=[("dve", 1)])

        with contextlib.ExitStack() as s12:
            K_res = s12.enter_context(nc.sbuf_tensor("K_res", [128, 8, T], BF16))
            V_res = s12.enter_context(nc.sbuf_tensor("V_res", [128, NCH, 1040], BF16))

            with contextlib.ExitStack() as p1:
                wkT = p1.enter_context(nc.sbuf_tensor("wkT", [128, 8, D], BF16))
                wvT = p1.enter_context(nc.sbuf_tensor("wvT", [128, 8, D], BF16))
                xbuf = [p1.enter_context(nc.sbuf_tensor(f"xbuf{i}", [128, D], F32)) for i in range(2)]
                tC = [p1.enter_context(nc.sbuf_tensor(f"tC{i}", [128, D], BF16)) for i in range(2)]
                tS = [p1.enter_context(nc.sbuf_tensor(f"tS{i}", [128, D], BF16)) for i in range(2)]
                xnb = [p1.enter_context(nc.sbuf_tensor(f"xnb{i}", [128, D], BF16)) for i in range(2)]
                xstg = [p1.enter_context(nc.sbuf_tensor(f"xstg{i}", [128, 8, 128], BF16)) for i in range(2)]
                krb = [p1.enter_context(nc.sbuf_tensor(f"krb{i}", [128, D], BF16)) for i in range(2)]
                t1b = p1.enter_context(nc.sbuf_tensor("t1b", [128, D], F32))
                t2b = p1.enter_context(nc.sbuf_tensor("t2b", [128, D], F32))
                khd = p1.enter_context(nc.sbuf_tensor("khd", [128, 16], F32))
                Mk2 = p1.enter_context(nc.sbuf_tensor("Mk2", [128, 16], F32))
                rcol = p1.enter_context(nc.sbuf_tensor("rcol", [128, 1], F32))
                mcol = p1.enter_context(nc.sbuf_tensor("mcol", [128, 1], F32))
                vcol = p1.enter_context(nc.sbuf_tensor("vcol", [128, 1], F32))
                rscol = p1.enter_context(nc.sbuf_tensor("rscol", [128, 1], F32))
                rstdc = p1.enter_context(nc.sbuf_tensor("rstdc", [128, 1], F32))
                mk2c = p1.enter_context(nc.sbuf_tensor("mk2c", [16, 1], F32))

                pk = p1.enter_context(nc.psum_tensor("pk", [128, D], F32))
                pv = p1.enter_context(nc.psum_tensor("pv", [128, D], F32))
                pxnT = p1.enter_context(nc.psum_tensor("pxnT", [128, D], BF16))
                pkT = p1.enter_context(nc.psum_tensor("pkT", [128, D], BF16))
                pw = p1.enter_context(nc.psum_tensor("pw", [128, D], BF16))

                p.op("gpsimd", lambda g: g.dma_start(out=identb[:], in_=identb_h), incs=[("lw", 16)])
                p.op("gpsimd", lambda g: g.dma_start(out=identf[:], in_=identf_h), incs=[("lw", 16)])
                p.op("gpsimd", lambda g: g.dma_start(out=onesb[:], in_=onesb_h), incs=[("lw", 16)])
                p.op("gpsimd", lambda g: g.dma_start(out=onesf[:], in_=onesf_h), incs=[("lw", 16)])
                p.op("vector", lambda v: v.memset(epscol[:], LN_EPS), incs=[("dve", 1)])
                p.op("vector", lambda v: v.memset(Mk2[:], 0.0), incs=[("dve", 1)])
                ones_view = V_res[:].rearrange("p c (h e) -> p c h e", e=65)[:, :, :, 64:65]
                p.op("vector", lambda v: v.memset(ones_view, 1.0), incs=[("dve", 1)])

                with nc.sbuf_tensor("wstg", [128, 4, D], BF16) as wstg:
                    prep_weight(wk_h.rearrange("(a p) d -> p a d", p=128), 8, wkT, 8, wstg, pw)
                    prep_weight(wv_h.rearrange("(a p) d -> p a d", p=128), 8, wvT, 8, wstg, pw)
                pe_prep = p.tok("pe")
                dve_prep = p.tok("dve")

                lx = [0, 0]
                lt = [0, 0]
                spx2 = [0, 0]
                xn_tok = 0
                xnT_toks = [0, 0]
                kT_toks = [0, 0]
                sqrt_tok = [0, 0]
                rope_tok = [0, 0]
                evx_toks = [0, 0]
                mv_tok = [0, 0]
                evv_tok = 0
                evk_tok = 0
                for c in range(NCH):
                    sl = c % 2
                    xw = [("dve", xn_tok), ("act", sqrt_tok[sl])]
                    if c < 2:
                        xw += [("pe", pe_prep), ("dve", dve_prep)]
                    p.op("gpsimd", lambda g, c=c, sl=sl: g.dma_start(out=xbuf[sl][:], in_=x_h[c * 128:(c + 1) * 128, :]),
                         waits=xw, incs=[(f"lx{sl}", 16)])
                    lx[sl] += 16
                    p.op("gpsimd", lambda g, c=c, sl=sl: g.dma_start(out=tC[sl][:], in_=tabC_h[c * 128:(c + 1) * 128, :]),
                         waits=[("dve", rope_tok[sl])], incs=[(f"lt{sl}", 16)])
                    p.op("gpsimd", lambda g, c=c, sl=sl: g.dma_start(out=tS[sl][:], in_=tabS_h[c * 128:(c + 1) * 128, :]),
                         incs=[(f"lt{sl}", 16)])
                    lt[sl] += 32
                    p.op("vector", lambda v, sl=sl: v.reduce_sum(rcol[:], xbuf[sl][:], axis=AX.X),
                         waits=[(f"lx{sl}", lx[sl])])
                    p.op("vector", lambda v: v.tensor_scalar_mul(mcol[:], rcol[:], INV_D))
                    tt = p.op("vector", lambda v, sl=sl: v.tensor_scalar_sub(xbuf[sl][:], xbuf[sl][:], mcol[:, 0:1]),
                              incs=[("dve", 1)])
                    xc_tok = tt["dve"]
                    p.op("scalar", lambda s, sl=sl: s.activation(t2b[:], xbuf[sl][:], AF.Square, accum_out=vcol[:, 0:1]),
                         waits=[("dve", xc_tok)])
                    tt = p.op("scalar", lambda s: s.activation(rscol[:], vcol[:, 0:1], AF.Sqrt, bias=epscol[:, 0:1], scale=INV_D),
                              incs=[("act", 1)])
                    sqrt_tok[sl] = tt["act"]
                    p.op("vector", lambda v: v.reciprocal(rstdc[:], rscol[:]), waits=[("act", sqrt_tok[sl])])
                    tt = p.op("vector", lambda v, sl=sl: v.tensor_scalar(xnb[sl][:], xbuf[sl][:], rstdc[:, 0:1], None, ALU.mult),
                              waits=[("pe", xnT_toks[sl])], incs=[("dve", 1)])
                    xn_tok = tt["dve"]
                    tt = p.op("tensor", lambda t_, sl=sl: [
                            t_.transpose(pxnT[:, j * 128:(j + 1) * 128], xnb[sl][:, j * 128:(j + 1) * 128], identb[:])
                            for j in range(8)][-1],
                            waits=[("dve", xn_tok), ("dve", evx_toks[1 - sl])], incs=[("pe", 1)])
                    xnT_toks[sl] = tt["pe"]
                    tt = p.op("vector", lambda v, sl=sl: v.tensor_copy(xstg[sl][:], pxnT[:].rearrange("p (j t) -> p j t", t=128)),
                              waits=[("pe", xnT_toks[sl]), ("pe", mv_tok[sl]), (f"spx{sl}", spx2[sl])], incs=[("dve", 1)])
                    evx_toks[sl] = tt["dve"]
                    p.op("sync", lambda s, c=c, sl=sl: s.dma_start(out=xn_sp[c], in_=xstg[sl][:]),
                         waits=[("dve", evx_toks[sl])], incs=[(f"spx{sl}", 16)])
                    spx2[sl] += 16
                    def kv_mms(t_, sl, wT, dst):
                        last = None
                        for osl in range(2):
                            for dc in range(8):
                                last = t_.matmul(dst[:, osl * 512:(osl + 1) * 512],
                                                 xstg[sl][:, dc, :],
                                                 wT[:, dc, osl * 512:(osl + 1) * 512],
                                                 start=(dc == 0), stop=(dc == 7))
                        return last
                    tt = p.op("tensor", lambda t_, sl=sl: kv_mms(t_, sl, wkT, pk),
                              waits=[("dve", evx_toks[sl]), ("dve", rope_tok[1 - sl])], incs=[("pe", 1)])
                    mk_tok = tt["pe"]
                    tt = p.op("tensor", lambda t_, sl=sl: kv_mms(t_, sl, wvT, pv),
                              waits=[("dve", evv_tok)], incs=[("pe", 1)])
                    mv_tok[sl] = tt["pe"]
                    p.op("vector", lambda v, sl=sl: v.tensor_tensor(t1b[:], pk[:], tC[sl][:], ALU.mult),
                         waits=[("pe", mk_tok), (f"lt{sl}", lt[sl]), ("act", sqrt_tok[sl])])
                    p.op("vector", lambda v, sl=sl: v.tensor_tensor(t2b[:, 0::2], pk[:, 1::2], tS[sl][:, 0::2], ALU.mult))
                    p.op("vector", lambda v, sl=sl: v.tensor_tensor(t2b[:, 1::2], pk[:, 0::2], tS[sl][:, 1::2], ALU.mult))
                    tt = p.op("vector", lambda v, sl=sl: v.tensor_add(krb[sl][:], t1b[:], t2b[:]),
                              waits=[("pe", kT_toks[sl])], incs=[("dve", 1)])
                    rope_tok[sl] = tt["dve"]
                    p.op("vector", lambda v, sl=sl: v.tensor_tensor(t1b[:], krb[sl][:], krb[sl][:], ALU.mult))
                    p.op("vector", lambda v: v.reduce_sum(khd[:].unsqueeze(2), t1b[:].rearrange("p (h e) -> p h e", e=64), axis=AX.X))
                    p.op("vector", lambda v: v.tensor_max(Mk2[:], Mk2[:], khd[:]))
                    vdst = V_res[:].rearrange("p c (h e) -> p c h e", e=65)
                    tt = p.op("vector", lambda v, c=c, vdst=vdst: v.tensor_copy(
                            vdst[:, c, :, 0:64], pv[:].rearrange("p (h e) -> p h e", e=64)),
                            waits=[("pe", mv_tok[sl])], incs=[("dve", 1)])
                    evv_tok = tt["dve"]
                    tt = p.op("tensor", lambda t_, sl=sl: [
                            t_.transpose(pkT[:, j * 128:(j + 1) * 128], krb[sl][:, j * 128:(j + 1) * 128], identb[:])
                            for j in range(8)][-1],
                            waits=[("dve", rope_tok[sl]), ("dve", evk_tok)], incs=[("pe", 1)])
                    kT_toks[sl] = tt["pe"]
                    tt = p.op("vector", lambda v, c=c: v.tensor_copy(
                            K_res[:][:, :, c * 128:(c + 1) * 128],
                            pkT[:].rearrange("p (j t) -> p j t", t=128)),
                            waits=[("pe", tt["pe"])], incs=[("dve", 1)])
                    evk_tok = tt["dve"]

                pmk = pw[:].bitcast(F32)[0:16, 0:128]
                tt = p.op("tensor", lambda t_: t_.transpose(pmk, Mk2[:, 0:16], identf[:]),
                          waits=[("dve", p.tok("dve"))], incs=[("pe", 1)])
                tt = p.op("vector", lambda v: v.reduce_max(mk2c[:], pmk, axis=AX.X),
                          waits=[("pe", tt["pe"])], incs=[("dve", 1)])
                p.op("vector", lambda v: v.tensor_copy(Mkcol[:], mk2c[:]), incs=[("dve", 1)])
                if "kv" in dbg:
                    p.op("sync", lambda s: s.dma_start(out=k_dbg, in_=K_res[:]),
                         waits=[("dve", p.tok("dve"))], incs=[("out0", 16)])
                    p.op("sync", lambda s: s.dma_start(out=v_dbg, in_=V_res[:]),
                         incs=[("out0", 16)])
                if "wt" in dbg:
                    p.op("sync", lambda s: s.dma_start(out=wkT_dbg, in_=wkT[:]),
                         waits=[("dve", p.tok("dve"))], incs=[("out0", 16)])
                    p.op("sync", lambda s: s.dma_start(out=wvT_dbg, in_=wvT[:]),
                         incs=[("out0", 16)])
                p.barrier()
                p.emit_phase(nc)

            with contextlib.ExitStack() as p2:
                wqT = p2.enter_context(nc.sbuf_tensor("wqT", [128, 8, D], BF16))
                maskt = p2.enter_context(nc.sbuf_tensor("maskt", [128, 4, 512], BF16))
                xnq = [p2.enter_context(nc.sbuf_tensor(f"xnq{i}", [128, 8, 128], BF16)) for i in range(2)]
                tC2 = [p2.enter_context(nc.sbuf_tensor(f"tC2{i}", [128, D], BF16)) for i in range(2)]
                tS2 = [p2.enter_context(nc.sbuf_tensor(f"tS2{i}", [128, D], BF16)) for i in range(2)]
                qrb = p2.enter_context(nc.sbuf_tensor("qrb", [128, D], BF16))
                Qfm = p2.enter_context(nc.sbuf_tensor("Qfm", [128, 8, 512], BF16))
                t1q = p2.enter_context(nc.sbuf_tensor("t1q", [128, D], F32))
                t2q = p2.enter_context(nc.sbuf_tensor("t2q", [128, D], F32))
                qhd = p2.enter_context(nc.sbuf_tensor("qhd", [128, 16], F32))
                Mq2 = p2.enter_context(nc.sbuf_tensor("Mq2", [128, 16], F32))
                mq2c = p2.enter_context(nc.sbuf_tensor("mq2c", [16, 1], F32))
                mqc = p2.enter_context(nc.sbuf_tensor("mqc", [16, 1], F32))
                b16 = p2.enter_context(nc.sbuf_tensor("b16", [16, 1], F32))
                bnr = p2.enter_context(nc.sbuf_tensor("bnr", [16, 1], I32))
                btb = p2.enter_context(nc.sbuf_tensor("btb", [16, 1], F32))
                brow = p2.enter_context(nc.sbuf_tensor("brow", [1, 16], F32))
                bpair = p2.enter_context(nc.sbuf_tensor("bpair", [1, 8], F32))
                bcols = p2.enter_context(nc.sbuf_tensor("bcols", [128, 8], F32))
                ptb = [p2.enter_context(nc.sbuf_tensor(f"ptb{i}", [128, 1024], BF16)) for i in range(3)]
                recL = [p2.enter_context(nc.sbuf_tensor(f"recL{i}", [1, 512], BF16)) for i in range(2)]
                onesr = p2.enter_context(nc.sbuf_tensor("onesr", [1, 512], F32))
                recLf = p2.enter_context(nc.sbuf_tensor("recLf", [1, 512], F32))
                Osb = p2.enter_context(nc.sbuf_tensor("Osb", [64, 512], F32))
                Ostg = p2.enter_context(nc.sbuf_tensor("Ostg", [128, 8, 512], BF16))

                pS = [p2.enter_context(nc.psum_tensor(f"pS{i}", [128, 1024], F32)) for i in range(2)]
                pO = [[p2.enter_context(nc.psum_tensor(f"pO{i}{hh}", [128, 512], F32)) for hh in range(2)] for i in range(2)]
                pq0, pq1 = pO[0][0], pO[0][1]

                p.op("gpsimd", lambda g: g.dma_start(out=maskt[:], in_=masks_h.rearrange("r p n -> p r n")),
                     incs=[("lw", 16)])
                p.op("vector", lambda v: v.memset(onesr[:], 1.0), incs=[("dve", 1)])
                pwq = pq0[:].bitcast(BF16)[:, 0:1024]
                with nc.sbuf_tensor("wstg2", [128, 2, D], BF16) as wstg2:
                    prep_weight(wq_h.rearrange("(a p) d -> p a d", p=128), 8, wqT, 8, wstg2, pwq)
                pe_prep2 = p.tok("pe")
                dve_prep2 = p.tok("dve")

                lxn = 0
                lq = [0, 0]
                spo = 0
                rope2_tok = [0, 0]
                qT_tok = 0
                evq_tok = 0
                mmq_last = 0
                oev_act = 0
                oev_dve = 0
                oev_set = [(0, 0), (0, 0)]
                oev_mult = 0
                lb_tok = 0
                for j in range(NQT):
                    for ts in range(4):
                        sl = ts % 2
                        cidx = 4 * j + ts
                        qsl = cidx % 2
                        if not (ts == 0 and j > 0):
                            xw = [("pe", mmq_last), ("spx0", 256), ("spx1", 256)]
                            if j == 0 and ts == 0:
                                xw += [("pe", pe_prep2), ("dve", dve_prep2)]
                            p.op("gpsimd", lambda g, cidx=cidx, qsl=qsl: g.dma_start(out=xnq[qsl][:], in_=xn_sp[cidx]),
                                 waits=xw, incs=[("lxn", 16)])
                            lxn += 16
                        if not (ts == 0 and j > 0):
                            p.op("gpsimd", lambda g, cidx=cidx, sl=sl: g.dma_start(out=tC2[sl][:], in_=tabC_h[cidx * 128:(cidx + 1) * 128, :]),
                                 waits=[("dve", rope2_tok[sl])], incs=[(f"lq{sl}", 16)])
                            p.op("gpsimd", lambda g, cidx=cidx, sl=sl: g.dma_start(out=tS2[sl][:], in_=tabS_h[cidx * 128:(cidx + 1) * 128, :]),
                                 incs=[(f"lq{sl}", 16)])
                            lq[sl] += 32
                        waits = [("lxn", lxn), ("dve", evq_tok), ("dve", rope2_tok[1 - sl])]
                        if ts == 0:
                            waits += [("dve", oev_dve)]
                        def q_mms(t_, qsl=qsl):
                            last = None
                            for osl, dst in ((0, pq0), (1, pq1)):
                                for dc in range(8):
                                    last = t_.matmul(dst[:],
                                                     xnq[qsl][:, dc, :],
                                                     wqT[:, dc, osl * 512:(osl + 1) * 512],
                                                     start=(dc == 0), stop=(dc == 7))
                            return last
                        tt = p.op("tensor", q_mms, waits=waits, incs=[("pe", 1)])
                        mmq_last = tt["pe"]
                        for half, psrc in ((0, pq0), (1, pq1)):
                            h0 = half * 512
                            p.op("vector", lambda v, sl=sl, psrc=psrc, h0=h0: v.tensor_tensor(
                                    t1q[:, h0:h0 + 512], psrc[:], tC2[sl][:, h0:h0 + 512], ALU.mult),
                                 waits=([("pe", mmq_last), (f"lq{sl}", lq[sl])] if half == 0 else []))
                            p.op("vector", lambda v, sl=sl, psrc=psrc, h0=h0: v.tensor_tensor(
                                    t2q[:, h0:h0 + 512][:, 0::2], psrc[:, 1::2], tS2[sl][:, h0:h0 + 512][:, 0::2], ALU.mult))
                            p.op("vector", lambda v, sl=sl, psrc=psrc, h0=h0: v.tensor_tensor(
                                    t2q[:, h0:h0 + 512][:, 1::2], psrc[:, 0::2], tS2[sl][:, h0:h0 + 512][:, 1::2], ALU.mult))
                        tt = p.op("vector", lambda v: v.tensor_add(qrb[:], t1q[:], t2q[:]),
                                  waits=[("pe", qT_tok)], incs=[("dve", 1)])
                        rope2_tok[sl] = tt["dve"]
                        p.op("vector", lambda v: v.tensor_tensor(t1q[:], qrb[:], qrb[:], ALU.mult))
                        p.op("vector", lambda v: v.reduce_sum(qhd[:].unsqueeze(2), t1q[:].rearrange("p (h e) -> p h e", e=64), axis=AX.X))
                        if ts == 0:
                            p.op("vector", lambda v: v.tensor_copy(Mq2[:], qhd[:]))
                        else:
                            p.op("vector", lambda v: v.tensor_max(Mq2[:], Mq2[:], qhd[:]))
                        tt = p.op("tensor", lambda t_: [
                                t_.transpose(pwq[:, jj * 128:(jj + 1) * 128], qrb[:, jj * 128:(jj + 1) * 128], identb[:])
                                for jj in range(8)][-1],
                                waits=[("dve", rope2_tok[sl])], incs=[("pe", 1)])
                        qT_tok = tt["pe"]
                        tt = p.op("vector", lambda v, ts=ts: v.tensor_copy(
                                Qfm[:, :, ts * 128:(ts + 1) * 128],
                                pwq[:].rearrange("p (jj t) -> p jj t", t=128)),
                                waits=[("pe", tt["pe"])], incs=[("dve", 1)])
                        evq_tok = tt["dve"]

                    tt = p.op("tensor", lambda t_: t_.transpose(pq0[:].bitcast(F32)[0:16, 0:128], Mq2[:, 0:16], identf[:]),
                              waits=[("dve", p.tok("dve"))], incs=[("pe", 1)])
                    tt = p.op("vector", lambda v: v.reduce_max(mq2c[:], pq0[0:16, 0:128], axis=AX.X),
                              waits=[("pe", tt["pe"])], incs=[("dve", 1)])
                    p.op("vector", lambda v: v.tensor_mul(b16[:], mq2c[:], Mkcol[:]))
                    ynr = bnr[:].bitcast(F32)
                    p.op("vector", lambda v: v.tensor_scalar(bnr[:], b16[:].bitcast(I32), 1, None, ALU.arith_shift_right))
                    p.op("vector", lambda v: v.tensor_scalar(bnr[:], bnr[:], -1, 0x5F3759DF, ALU.mult, ALU.add))
                    for _nr in range(2):
                        p.op("vector", lambda v: v.tensor_mul(btb[:], b16[:], ynr))
                        p.op("vector", lambda v: v.tensor_mul(btb[:], btb[:], ynr))
                        p.op("vector", lambda v: v.tensor_scalar(btb[:], btb[:], -0.5, 1.5, ALU.mult, ALU.add))
                        p.op("vector", lambda v: v.tensor_mul(ynr, ynr, btb[:]))
                    p.op("vector", lambda v: v.tensor_mul(b16[:], b16[:], ynr))
                    tt = p.op("vector", lambda v: v.tensor_scalar(b16[:], b16[:], -1.01, -0.5, ALU.mult, ALU.add),
                              incs=[("dve", 1)])
                    tt = p.op("tensor", lambda t_: t_.transpose(pq0[:].bitcast(F32)[0:1, 0:16], b16[:], identf[0:16, 0:16]),
                              waits=[("dve", tt["dve"])], incs=[("pe", 1)])
                    p.op("vector", lambda v: v.tensor_copy(brow[:], pq0[0:1, 0:16]),
                         waits=[("pe", tt["pe"])])
                    tt = p.op("vector", lambda v: v.tensor_tensor(bpair[:], brow[:, 0::2], brow[:, 1::2], ALU.min),
                              incs=[("dve", 1)])
                    tt = p.op("tensor", lambda t_: t_.matmul(pq0[:, 0:8], onesf[:], bpair[:], start=True, stop=True),
                              waits=[("dve", tt["dve"])], incs=[("pe", 1)])
                    tt = p.op("vector", lambda v: v.tensor_copy(bcols[:], pq0[:, 0:8]),
                              waits=[("pe", tt["pe"])], incs=[("dve", 1)])
                    bc_tok = tt["dve"]

                    nch_j = 4 * (j + 1)
                    insts = [(pr, c) for pr in range(8) for c in range(nch_j)]
                    exp_toks = {}
                    av_toks = {}
                    mask_toks = {}
                    evac_jobs = {}

                    def emit_s_exp(i):
                        pr, c = insts[i]
                        sl2 = i % 2
                        sl3 = i % 3
                        dr = c - 4 * j
                        def s_mms(t_, pr=pr, c=c, sl2=sl2):
                            t_.matmul(pS[sl2][:, 0:512], K_res[0:64, pr, c * 128:(c + 1) * 128],
                                      Qfm[0:64, pr, :], start=True, stop=True)
                            return t_.matmul(pS[sl2][:, 512:1024], K_res[64:128, pr, c * 128:(c + 1) * 128],
                                             Qfm[64:128, pr, :], start=True, stop=True,
                                             tile_position=(64, 0))
                        tt = p.op("tensor", s_mms,
                                  waits=[("act", exp_toks.get(i - 2, 0)), ("dve", bc_tok)],
                                  incs=[("pe", 1)])
                        tt = p.op("scalar", lambda s, pr=pr, sl3=sl3, sl2=sl2: s.activation(
                                ptb[sl3][:], pS[sl2][:], AF.Exp, bias=bcols[:, pr:pr + 1]),
                                waits=[("pe", tt["pe"]), ("pe", av_toks.get(i - 3, 0))],
                                incs=[("act", 1)])
                        exp_toks[i] = tt["act"]
                        if dr >= 0:
                            p.op("vector", lambda v, dr=dr, sl3=sl3: v.tensor_tensor(
                                    ptb[sl3][:, 0:512], ptb[sl3][:, 0:512], maskt[:, dr, :], ALU.mult),
                                    waits=[("act", exp_toks[i])])
                            tt = p.op("vector", lambda v, dr=dr, sl3=sl3: v.tensor_tensor(
                                    ptb[sl3][:, 512:1024], ptb[sl3][:, 512:1024], maskt[:, dr, :], ALU.mult),
                                    incs=[("dve", 1)])
                            mask_toks[i] = tt["dve"]

                    def emit_av(i):
                        nonlocal lb_tok
                        pr, c = insts[i]
                        sl2 = i % 2
                        st2 = pr % 2
                        waits = [("act", exp_toks[i])]
                        if i in mask_toks:
                            waits.append(("dve", mask_toks[i]))
                        if c == 0:
                            waits += [("dve", oev_set[st2][0])]
                        sl3 = i % 3
                        def av_mms(t_, pr=pr, c=c, sl3=sl3, st2=st2, first=(c == 0), last=(c == nch_j - 1)):
                            t_.matmul(pO[st2][0][0:65, :], V_res[:, c, 130 * pr:130 * pr + 65],
                                      ptb[sl3][:, 0:512], start=first, stop=last)
                            return t_.matmul(pO[st2][1][0:65, :], V_res[:, c, 130 * pr + 65:130 * pr + 130],
                                             ptb[sl3][:, 512:1024], start=first, stop=last)
                        tt = p.op("tensor", av_mms, waits=waits, incs=[("pe", 1)])
                        av_toks[i] = tt["pe"]
                        if c == nch_j - 1:
                            rec = emit_evac_early(pr, av_toks[i])
                            pending.append((i, pr, av_toks[i], rec))

                    def emit_evac_early(pr, av_last):
                        nonlocal lb_tok
                        rec = []
                        for hh in range(2):
                            bank = pO[pr % 2][hh]
                            p.op("vector", lambda v, bank=bank: v.reciprocal(recLf[:], bank[64:65, :]),
                                 waits=[("pe", av_last)])
                            tt = p.op("vector", lambda v, hh=hh: v.tensor_copy(recL[hh][:], recLf[:]),
                                      waits=[("pe", lb_tok)], incs=[("dve", 1)])
                            rec.append(tt["dve"])
                        return rec

                    def emit_evac_late(pr, av_last, rec):
                        nonlocal lb_tok, oev_mult
                        st2 = pr % 2
                        dv = 0
                        for hh in range(2):
                            bank = pO[st2][hh]
                            tt = p.op("vector", lambda v, bank=bank: v.tensor_copy(Osb[:], bank[0:64, :]),
                                      waits=[("dve", oev_mult)], incs=[("dve", 1)])
                            ac = tt["dve"]
                            tt = p.op("tensor", lambda t_, bank=bank, hh=hh: t_.matmul(bank[0:64, :], onesb[:], recL[hh][:], start=True, stop=True),
                                      waits=[("dve", max(rec[hh], ac))], incs=[("pe", 1)])
                            lb_tok = tt["pe"]
                            tt = p.op("vector", lambda v, hh=hh, pr=pr, bank=bank: v.tensor_tensor(
                                    Ostg[hh * 64:(hh + 1) * 64, pr, :], Osb[:], bank[0:64, :], ALU.mult),
                                    waits=[("pe", lb_tok), ("spo0", spo)],
                                    incs=[("dve", 1)])
                            dv = tt["dve"]
                            oev_mult = dv
                        oev_set[st2] = (dv, dv)

                    pending = []
                    n_inst = len(insts)
                    emit_s_exp(0)
                    if n_inst > 1:
                        emit_s_exp(1)
                    pf_i = max(2, (6 * n_inst) // 10)
                    for i in range(2, n_inst):
                        emit_s_exp(i)
                        while pending and pending[0][0] <= i - 3:
                            job = pending.pop(0)
                            emit_evac_late(job[1], job[2], job[3])
                        emit_av(i - 2)
                        if i == pf_i and j + 1 < NQT:
                            cnext = 4 * (j + 1)
                            p.op("sync", lambda g, cnext=cnext: g.dma_start(out=xnq[cnext % 2][:], in_=xn_sp[cnext]),
                                 waits=[("pe", mmq_last)], incs=[("lxn", 16)])
                            lxn += 16
                            p.op("gpsimd", lambda g, cnext=cnext: g.dma_start(out=tC2[0][:], in_=tabC_h[cnext * 128:(cnext + 1) * 128, :]),
                                 waits=[("dve", rope2_tok[0])], incs=[("lq0", 16)])
                            p.op("gpsimd", lambda g, cnext=cnext: g.dma_start(out=tS2[0][:], in_=tabS_h[cnext * 128:(cnext + 1) * 128, :]),
                                 incs=[("lq0", 16)])
                            lq[0] += 32
                    if n_inst > 1:
                        emit_av(n_inst - 2)
                    emit_av(n_inst - 1)
                    for job in pending:
                        emit_evac_late(job[1], job[2], job[3])
                    pending = []
                    pending = []
                    oev_dve = max(oev_set[0][0], oev_set[1][0])
                    oev_act = oev_dve
                    for cc in range(4):
                        p.op("sync", lambda s, j=j, cc=cc: s.dma_start(
                                out=o_sp[4 * j + cc], in_=Ostg[:, :, cc * 128:(cc + 1) * 128]),
                             waits=[("dve", oev_dve)], incs=[("spo0", 16)])
                    spo += 64
                p.barrier()
                p.emit_phase(nc)

        with contextlib.ExitStack() as p3:
            wgT = p3.enter_context(nc.sbuf_tensor("wgT", [128, 8, HID], BF16))
            wuT = p3.enter_context(nc.sbuf_tensor("wuT", [128, 8, HID], BF16))
            woT = p3.enter_context(nc.sbuf_tensor("woT", [128, 8, D], BF16))
            wdT = p3.enter_context(nc.sbuf_tensor("wdT", [128, 24, D], BF16))

            with nc.sbuf_tensor("wstg3", [128, 16, D], BF16) as wstg3, \
                 nc.psum_tensor("pw3", [128, D], BF16) as pw3:
                prep_weight(wg_h.rearrange("(a p) d -> p a d", p=128), 24, wgT, 8, wstg3, pw3)
                prep_weight(wu_h.rearrange("(a p) d -> p a d", p=128), 24, wuT, 8, wstg3, pw3)
                prep_weight(wo_h.rearrange("(a p) d -> p a d", p=128), 8, woT, 8, wstg3, pw3)
            with nc.sbuf_tensor("wstg4", [128, 2, HID], BF16) as wstg4, \
                 nc.psum_tensor("pw4", [128, D], BF16) as pw4:
                srcq = wd_h.rearrange("(a p) h -> p a h", p=128)
                for q in range(4):
                    p.op("gpsimd", lambda g, q=q: g.dma_start(out=wstg4[:], in_=srcq[:, 2 * q:2 * q + 2, :]),
                         waits=[("pe", p.tok("pe")), ("lw", p.tok("lw"))], incs=[("lw", 16)])
                    lwtok = p.tok("lw")
                    for a in range(2):
                        dc = 2 * q + a
                        for b in range(3):
                            for hc in range(8):
                                hcg = b * 8 + hc
                                p.op("tensor", lambda t_, a=a, hcg=hcg, hc=hc: t_.transpose(
                                        pw4[:, hc * 128:(hc + 1) * 128],
                                        wstg4[:, a, hcg * 128:(hcg + 1) * 128], identb[:]),
                                     waits=([("lw", lwtok), ("dve", p.tok("dve"))] if hc == 0 else []),
                                     incs=([("pe", 1)] if hc == 7 else []))
                            wdst = wdT[:][:, b * 8:(b + 1) * 8, dc * 128:(dc + 1) * 128]
                            p.op("vector", lambda v, wdst=wdst: v.tensor_copy(
                                    wdst, pw4[:].rearrange("p (hh t) -> p hh t", t=128)),
                                 waits=[("pe", p.tok("pe"))], incs=[("dve", 1)])
            pe_prep3 = p.tok("pe")
            dve_prep3 = p.tok("dve")

            xn3 = [p3.enter_context(nc.sbuf_tensor(f"xn3{i}", [128, 8, 128], BF16)) for i in range(2)]
            O3 = [p3.enter_context(nc.sbuf_tensor(f"O3{i}", [128, 8, 128], BF16)) for i in range(2)]
            x3 = p3.enter_context(nc.sbuf_tensor("x3", [128, D], F32))
            htm = p3.enter_context(nc.sbuf_tensor("htm", [128, HID], BF16))
            hfm = p3.enter_context(nc.sbuf_tensor("hfm", [128, 24, 128], BF16))
            gsil = p3.enter_context(nc.sbuf_tensor("gsil", [128, 512], F32))
            gdn = p3.enter_context(nc.sbuf_tensor("gdn", [128, 512], F32))
            sgf = p3.enter_context(nc.sbuf_tensor("sgf", [128, 512], F32))
            outb = [p3.enter_context(nc.sbuf_tensor(f"outb{i}", [128, D], F32)) for i in range(2)]

            pA = p3.enter_context(nc.psum_tensor("pA", [128, 512], F32))
            pB = p3.enter_context(nc.psum_tensor("pB", [128, 512], F32))
            pC = p3.enter_context(nc.psum_tensor("pC", [128, 512], F32))
            pD = p3.enter_context(nc.psum_tensor("pD", [128, 512], F32))
            phT = p3.enter_context(nc.psum_tensor("phT", [128, 1024], BF16))
            pd0 = p3.enter_context(nc.psum_tensor("pd0", [128, 512], F32))
            pd1 = p3.enter_context(nc.psum_tensor("pd1", [128, 512], F32))
            psj0 = p3.enter_context(nc.psum_tensor("psj0", [128, 512], F32))

            l3a = [0, 0]
            l3b = [0, 0]
            mmgu_tok = {}
            silu_tok = {}
            hmul_tok = {}
            sg_tok = {}
            ep_tok = [0, 0]
            sj_tok = [0, 0]
            dn_tok = [0, 0]
            he_tok = 0
            outc = [0, 0]
            for ti in range(NCH):
                sl = ti % 2
                xw = [("pe", mmgu_tok.get((ti - 2, 5), 0))]
                ow = [("pe", sj_tok[sl]), ("spo0", 512)]
                if ti < 2:
                    xw += [("pe", pe_prep3), ("dve", dve_prep3)]
                    ow += [("pe", pe_prep3), ("dve", dve_prep3)]
                p.op("gpsimd", lambda g, ti=ti, sl=sl: g.dma_start(out=xn3[sl][:], in_=xn_sp[ti]),
                     waits=xw, incs=[(f"l3a{sl}", 16)])
                p.op("gpsimd", lambda g, ti=ti, sl=sl: g.dma_start(out=O3[sl][:], in_=o_sp[ti]),
                     waits=ow, incs=[(f"l3a{sl}", 16)])
                l3a[sl] += 32
                xw2 = [("dve", sg_tok.get((ti - 1, 1), 0))]
                if ti < 2:
                    xw2 += [("dve", dve_prep3)]
                p.op("sync", lambda s, ti=ti: s.dma_start(out=x3[:], in_=x_h[ti * 128:(ti + 1) * 128, :]),
                     waits=xw2, incs=[("l3b0", 16)])
                l3b[sl] += 16
                hT_sched = {3: 0, 4: 1, 5: 2}
                for hs in range(6):
                    pg = (pA, pB)[hs % 2]
                    pu = (pC, pD)[hs % 2]
                    if hs >= 2:
                        frees = [("act", silu_tok.get((ti, hs - 2), 0)), ("dve", hmul_tok.get((ti, hs - 2), 0))]
                    else:
                        frees = [("act", silu_tok.get((ti - 1, hs + 4), 0)),
                                 ("dve", hmul_tok.get((ti - 1, hs + 4), 0))]
                    def gu_mms(t_, sl=sl, hs=hs, pg=pg, pu=pu):
                        last = None
                        for wT, dst in ((wgT, pg), (wuT, pu)):
                            for dc in range(8):
                                last = t_.matmul(dst[:], xn3[sl][:, dc, :],
                                                 wT[:, dc, hs * 512:(hs + 1) * 512],
                                                 start=(dc == 0), stop=(dc == 7))
                        return last
                    tt = p.op("tensor", gu_mms, waits=[(f"l3a{sl}", l3a[sl])] + frees, incs=[("pe", 1)])
                    mmgu_tok[(ti, hs)] = tt["pe"]
                    tt = p.op("scalar", lambda s, pg=pg: s.activation(gsil[:], pg[:], AF.Silu, scale=FFN_S),
                              waits=[("pe", tt["pe"]),
                                     ("dve", hmul_tok.get((ti, hs - 1), 0) if hs >= 1 else hmul_tok.get((ti - 1, 5), 0))],
                              incs=[("act", 1)])
                    silu_tok[(ti, hs)] = tt["act"]
                    tt = p.op("vector", lambda v, hs=hs, pu=pu: v.tensor_tensor(
                            htm[:, hs * 512:(hs + 1) * 512], gsil[:], pu[:], ALU.mult),
                            waits=[("act", tt["act"])], incs=[("dve", 1)])
                    hmul_tok[(ti, hs)] = tt["dve"]
                for b in range(3):
                    tw = [("dve", hmul_tok[(ti, 2 * b + 1)]), ("dve", he_tok)]
                    if b == 0:
                        tw.append(("dve", sg_tok.get((ti - 1, 1), 0)))
                    tt = p.op("tensor", lambda t_, b=b: [
                            t_.transpose(phT[:, k * 128:(k + 1) * 128],
                                         htm[:, (b * 8 + k) * 128:(b * 8 + k + 1) * 128], identb[:])
                            for k in range(8)][-1],
                            waits=tw,
                            incs=[("pe", 1)])
                    tt = p.op("vector", lambda v, b=b: v.tensor_copy(
                            hfm[:, b * 8:(b + 1) * 8, :], phT[:].rearrange("p (k t) -> p k t", t=128)),
                            waits=[("pe", tt["pe"])], incs=[("dve", 1)])
                    he_tok = tt["dve"]
                for ds in range(2):
                    pd = (pd0, pd1)[ds]
                    psj = psj0 if ds == 0 else phT[:].bitcast(F32)[:, 0:512]
                    def dn_mms(t_, ds=ds, pd=pd):
                        last = None
                        for hc in range(24):
                            last = t_.matmul(pd[:], hfm[:, hc, :], wdT[:, hc, ds * 512:(ds + 1) * 512],
                                             start=(hc == 0), stop=(hc == 23))
                        return last
                    tt = p.op("tensor", dn_mms, waits=[("dve", he_tok), ("act", dn_tok[ds])], incs=[("pe", 1)])
                    pe_dn = tt["pe"]
                    def sj_mms(t_, ds=ds, psj=psj, sl=sl):
                        last = None
                        for oc in range(8):
                            last = t_.matmul(psj[:] if ds == 0 else psj, O3[sl][:, oc, :], woT[:, oc, ds * 512:(ds + 1) * 512],
                                             start=(oc == 0), stop=(oc == 7))
                        return last
                    sjw = [(f"l3a{sl}", l3a[sl])]
                    if ds == 0:
                        sjw.append(("dve", sg_tok.get((ti - 1, 0), 0)))
                    else:
                        sjw.append(("dve", he_tok))
                    tt = p.op("tensor", sj_mms, waits=sjw, incs=[("pe", 1)])
                    sj_tok[sl] = tt["pe"]
                    gprev = sg_tok.get((ti, 0), 0) if ds == 1 else sg_tok.get((ti - 1, 1), 0)
                    tt = p.op("scalar", lambda s, pd=pd: s.activation(gdn[:], pd[:], AF.Silu, scale=FFN_S2),
                              waits=[("pe", pe_dn), ("dve", gprev)], incs=[("act", 1)])
                    dn_tok[ds] = tt["act"]
                    p.op("vector", lambda v, psj=psj: v.tensor_tensor(sgf[:], gdn[:], psj[:], ALU.mult),
                         waits=[("act", dn_tok[ds]), ("pe", sj_tok[sl]), ("l3b0", sum(l3b)),
                                (f"out{sl}", outc[sl])])
                    tt = p.op("vector", lambda v, ds=ds, sl=sl: v.tensor_add(
                            outb[sl][:, ds * 512:(ds + 1) * 512], sgf[:], x3[:, ds * 512:(ds + 1) * 512]),
                            incs=[("dve", 1)])
                    sg_tok[(ti, ds)] = tt["dve"]
                    ep_tok[sl] = tt["dve"]
                p.op("sync", lambda s, ti=ti, sl=sl: s.dma_start(out=out_h[ti * 128:(ti + 1) * 128, :], in_=outb[sl][:]),
                     waits=[("dve", ep_tok[sl])], incs=[(f"out{sl}", 16)])
                outc[sl] += 16
            p.op("sync", lambda s: s.nop(), waits=[("out0", p.tok("out0")), ("out1", p.tok("out1"))])
            p.emit_phase(nc)

    return nc


_NC_CACHE = {}


def _get_nc(dbg=()):
    key = tuple(sorted(dbg))
    if key not in _NC_CACHE:
        _NC_CACHE[key] = build_nc(dbg)
    return _NC_CACHE[key]


def kernel(L0_attn_x, L0_attn_wq, L0_attn_wk, L0_attn_wv, L0_attn_wo,
           L0_ffn_w_gate, L0_ffn_w_up, L0_ffn_w_down):
    consts = _host_consts()
    nc = _get_nc()
    x = np.ascontiguousarray(np.asarray(L0_attn_x, dtype=np.float32))
    shared = dict(
        wq=np.ascontiguousarray(np.asarray(L0_attn_wq, np.float32)),
        wk=np.ascontiguousarray(np.asarray(L0_attn_wk, np.float32)),
        wv=np.ascontiguousarray(np.asarray(L0_attn_wv, np.float32)),
        wo=np.ascontiguousarray(np.asarray(L0_attn_wo, np.float32)),
        wg=np.ascontiguousarray(np.asarray(L0_ffn_w_gate, np.int32)),
        wu=np.ascontiguousarray(np.asarray(L0_ffn_w_up, np.int32)),
        wd=np.ascontiguousarray(np.asarray(L0_ffn_w_down, np.int32)),
        tabC=consts["tabC"], tabS=consts["tabS"], masks=consts["masks"],
        identb_h=consts["identb"], identf_h=consts["identf"],
        onesb_h=consts["onesb"], onesf_h=consts["onesf"],
    )
    in_maps = [dict(x=x[b], **shared) for b in range(8)]
    res = run_bass_kernel_spmd(nc, in_maps, core_ids=list(range(8)))
    return np.stack([r["out"] for r in res.results], axis=0)


# version 12
# speedup vs baseline: 1.0253x; 1.0253x over previous
import contextlib
import numpy as np
import ml_dtypes
import concourse.bass as bass
import concourse.mybir as mybir
from concourse.bass_utils import run_bass_kernel_spmd

F32 = mybir.dt.float32
BF16 = mybir.dt.bfloat16
I32 = mybir.dt.int32
AF = mybir.ActivationFunctionType
ALU = mybir.AluOpType
AX = mybir.AxisListType

T, D, H, HD, HID = 4096, 1024, 16, 64, 3072
NCH = T // 128
NQT = T // 512
QK_SCALE = float(HD) ** -0.25
FFN_S = float(np.float32(1.0) / np.float32(127.0))
FFN_S2 = float(np.float32(FFN_S) * np.float32(FFN_S))
INV_D = 1.0 / D
LN_EPS = 1e-5

_CONSTS = None


def _host_consts():
    global _CONSTS
    if _CONSTS is not None:
        return _CONSTS
    import jax
    import jax.numpy as jnp
    with jax.default_device(jax.devices("cpu")[0]):
        theta = jnp.exp(jnp.log(10000.0) * jnp.arange(HD // 2, dtype=jnp.float32) / HD)
        args = jnp.arange(T, dtype=jnp.float32)[:, None] * theta
        c = np.asarray(jnp.cos(args), dtype=np.float32)
        s = np.asarray(jnp.sin(args), dtype=np.float32)
    C64 = np.empty((T, HD), np.float32)
    C64[:, 0::2] = c
    C64[:, 1::2] = c
    Ss64 = np.empty((T, HD), np.float32)
    Ss64[:, 0::2] = -s
    Ss64[:, 1::2] = s
    tabC = np.tile(C64 * QK_SCALE, (1, H)).astype(ml_dtypes.bfloat16)
    tabS = np.tile(Ss64 * QK_SCALE, (1, H)).astype(ml_dtypes.bfloat16)
    pidx = np.arange(128)[:, None]
    fidx = np.arange(512)[None, :]
    masks = np.stack([(fidx >= (128 * r + pidx)) for r in range(4)]).astype(np.float32)
    masks = masks.astype(ml_dtypes.bfloat16)
    _CONSTS = dict(
        tabC=tabC, tabS=tabS, masks=masks,
        identb=np.eye(128, dtype=ml_dtypes.bfloat16),
        identf=np.eye(128, dtype=np.float32),
        onesb=np.ones((1, 64), dtype=ml_dtypes.bfloat16),
        onesf=np.ones((1, 128), dtype=np.float32),
    )
    return _CONSTS


class Plan:

    def __init__(self, sems):
        self.ops = []
        self.cnt = {}
        self.sems = sems
        self.lo = 0

    def tok(self, sem):
        return self.cnt.get(sem, 0)

    PARANOID = False
    SELFSYNC = True

    def op(self, eng, fn, waits=(), incs=(), selfsync=True):
        if Plan.PARANOID:
            waits = list(waits) + [(s, self.tok(s)) for s in self.cnt if s not in [i[0] for i in incs]]
        if Plan.SELFSYNC and selfsync and eng in ("vector", "scalar"):
            sem = "dve" if eng == "vector" else "act"
            waits = list(waits) + [(sem, self.tok(sem))]
            if not any(s == sem for s, _ in incs):
                incs = list(incs) + [(sem, 1)]
        waits = [(s, v) for (s, v) in waits if v > 0]
        for s, amt in incs:
            self.cnt[s] = self.cnt.get(s, 0) + amt
        self.ops.append((eng, fn, waits, list(incs)))
        return {s: self.cnt[s] for s, _ in incs}

    def barrier(self):
        base = self.tok("ph")
        for e in ("sync", "gpsimd", "vector", "scalar", "tensor"):
            self.op(e, lambda eng: eng.nop(), incs=[("ph", 1)], selfsync=False)
        for e in ("sync", "gpsimd", "vector", "scalar", "tensor"):
            self.op(e, lambda eng: eng.nop(), waits=[("ph", base + 5)], selfsync=False)

    def emit_phase(self, nc):
        ops = self.ops[self.lo:]
        self.lo = len(self.ops)
        with nc.Block() as block:
            def section(eng_name):
                def run(eng):
                    for e, fn, waits, incs in ops:
                        if e != eng_name:
                            continue
                        for s, v in waits:
                            eng.wait_ge(self.sems[s], v)
                        instr = fn(eng)
                        for s, amt in incs:
                            instr.then_inc(self.sems[s], amt)
                return run
            block.sync(section("sync"))
            block.gpsimd(section("gpsimd"))
            block.vector(section("vector"))
            block.scalar(section("scalar"))
            block.tensor(section("tensor"))


SEM_NAMES = (
    "pe", "dve", "act", "pool", "ph",
    "lx0", "lx1", "lt0", "lt1", "spx0", "spx1", "lw",
    "lxn", "lq0", "lq1", "spo0", "spo1",
    "l3a0", "l3a1", "l3b0", "l3b1", "out0", "out1",
)


def build_nc(dbg=()):
    nc = bass.Bass("TRN2", target_bir_lowering=False, debug=False,
                   detect_race_conditions=False)

    x_h = nc.dram_tensor("x", [T, D], F32, kind="ExternalInput").ap()
    wq_h = nc.dram_tensor("wq", [D, D], F32, kind="ExternalInput").ap()
    wk_h = nc.dram_tensor("wk", [D, D], F32, kind="ExternalInput").ap()
    wv_h = nc.dram_tensor("wv", [D, D], F32, kind="ExternalInput").ap()
    wo_h = nc.dram_tensor("wo", [D, D], F32, kind="ExternalInput").ap()
    wg_h = nc.dram_tensor("wg", [HID, D], I32, kind="ExternalInput").ap()
    wu_h = nc.dram_tensor("wu", [HID, D], I32, kind="ExternalInput").ap()
    wd_h = nc.dram_tensor("wd", [D, HID], I32, kind="ExternalInput").ap()
    tabC_h = nc.dram_tensor("tabC", [T, D], BF16, kind="ExternalInput").ap()
    tabS_h = nc.dram_tensor("tabS", [T, D], BF16, kind="ExternalInput").ap()
    masks_h = nc.dram_tensor("masks", [4, 128, 512], BF16, kind="ExternalInput").ap()
    identb_h = nc.dram_tensor("identb_h", [128, 128], BF16, kind="ExternalInput").ap()
    identf_h = nc.dram_tensor("identf_h", [128, 128], F32, kind="ExternalInput").ap()
    onesb_h = nc.dram_tensor("onesb_h", [1, 64], BF16, kind="ExternalInput").ap()
    onesf_h = nc.dram_tensor("onesf_h", [1, 128], F32, kind="ExternalInput").ap()
    out_h = nc.dram_tensor("out", [T, D], F32, kind="ExternalOutput").ap()

    xn_kind = "ExternalOutput"
    o_kind = "ExternalOutput"
    xn_sp = nc.dram_tensor("xn_sp", [NCH, 128, 8, 128], BF16, kind=xn_kind).ap()
    o_sp = nc.dram_tensor("o_sp", [NCH, 128, 8, 128], BF16, kind=o_kind).ap()
    if "kv" in dbg:
        k_dbg = nc.dram_tensor("k_dbg", [128, 8, T], BF16, kind="ExternalOutput").ap()
        v_dbg = nc.dram_tensor("v_dbg", [128, NCH, 1040], BF16, kind="ExternalOutput").ap()
    if "wt" in dbg:
        wkT_dbg = nc.dram_tensor("wkT_dbg", [128, 8, D], BF16, kind="ExternalOutput").ap()
        wvT_dbg = nc.dram_tensor("wvT_dbg", [128, 8, D], BF16, kind="ExternalOutput").ap()

    with contextlib.ExitStack() as st:
        sems = {n: st.enter_context(nc.semaphore(n)) for n in SEM_NAMES}
        p = Plan(sems)

        identb = st.enter_context(nc.sbuf_tensor("identb", [128, 128], BF16))
        identf = st.enter_context(nc.sbuf_tensor("identf", [128, 128], F32))
        onesb = st.enter_context(nc.sbuf_tensor("onesb", [1, 64], BF16))
        onesf = st.enter_context(nc.sbuf_tensor("onesf", [1, 128], F32))
        epscol = st.enter_context(nc.sbuf_tensor("epscol", [128, 1], F32))
        Mkcol = st.enter_context(nc.sbuf_tensor("Mkcol", [16, 1], F32))

        def prep_weight(w_src, rows, wT, nblk, stage, pwps):
            seg = stage.shape[1]
            for s0 in range(0, rows, seg):
                ns = min(seg, rows - s0)
                p.op("gpsimd",
                     lambda g, s0=s0, ns=ns: g.dma_start(out=stage[:, 0:ns, :], in_=w_src[:, s0:s0 + ns, :]),
                     waits=[("pe", p.tok("pe")), ("lw", p.tok("lw"))], incs=[("lw", 16)])
                lwtok = p.tok("lw")
                for blk in range(nblk):
                    for gl in range(0, ns, 8):
                        ng = min(8, ns - gl)
                        for al in range(gl, gl + ng):
                            p.op("tensor",
                                 lambda t_, al=al, blk=blk, gl=gl: t_.transpose(
                                     pwps[:, (al - gl) * 128:(al - gl + 1) * 128],
                                     stage[:, al, blk * 128:(blk + 1) * 128], identb[:]),
                                 waits=([("lw", lwtok), ("dve", p.tok("dve"))] if al == gl else []),
                                 incs=([("pe", 1)] if al == gl + ng - 1 else []))
                        p.op("vector",
                             lambda v, blk=blk, s0=s0, gl=gl, ng=ng: v.tensor_copy(
                                 wT[:, blk, (s0 + gl) * 128:(s0 + gl + ng) * 128],
                                 pwps[:, 0:ng * 128]),
                             waits=[("pe", p.tok("pe"))], incs=[("dve", 1)])

        with contextlib.ExitStack() as s12:
            K_res = s12.enter_context(nc.sbuf_tensor("K_res", [128, 8, T], BF16))
            V_res = s12.enter_context(nc.sbuf_tensor("V_res", [128, NCH, 1040], BF16))

            with contextlib.ExitStack() as p1:
                wkT = p1.enter_context(nc.sbuf_tensor("wkT", [128, 8, D], BF16))
                wvT = p1.enter_context(nc.sbuf_tensor("wvT", [128, 8, D], BF16))
                xbuf = [p1.enter_context(nc.sbuf_tensor(f"xbuf{i}", [128, D], F32)) for i in range(2)]
                tC = [p1.enter_context(nc.sbuf_tensor(f"tC{i}", [128, D], BF16)) for i in range(2)]
                tS = [p1.enter_context(nc.sbuf_tensor(f"tS{i}", [128, D], BF16)) for i in range(2)]
                xnb = [p1.enter_context(nc.sbuf_tensor(f"xnb{i}", [128, D], BF16)) for i in range(2)]
                xstg = [p1.enter_context(nc.sbuf_tensor(f"xstg{i}", [128, 8, 128], BF16)) for i in range(2)]
                krb = [p1.enter_context(nc.sbuf_tensor(f"krb{i}", [128, D], BF16)) for i in range(2)]
                t1b = p1.enter_context(nc.sbuf_tensor("t1b", [128, D], F32))
                t2b = p1.enter_context(nc.sbuf_tensor("t2b", [128, D], F32))
                khd = p1.enter_context(nc.sbuf_tensor("khd", [128, 16], F32))
                Mk2 = p1.enter_context(nc.sbuf_tensor("Mk2", [128, 16], F32))
                rcol = p1.enter_context(nc.sbuf_tensor("rcol", [128, 1], F32))
                mcol = p1.enter_context(nc.sbuf_tensor("mcol", [128, 1], F32))
                vcol = p1.enter_context(nc.sbuf_tensor("vcol", [128, 1], F32))
                rscol = p1.enter_context(nc.sbuf_tensor("rscol", [128, 1], F32))
                rstdc = p1.enter_context(nc.sbuf_tensor("rstdc", [128, 1], F32))
                mk2c = p1.enter_context(nc.sbuf_tensor("mk2c", [16, 1], F32))

                pk = p1.enter_context(nc.psum_tensor("pk", [128, D], F32))
                pv = p1.enter_context(nc.psum_tensor("pv", [128, D], F32))
                pxnT = p1.enter_context(nc.psum_tensor("pxnT", [128, D], BF16))
                pkT = p1.enter_context(nc.psum_tensor("pkT", [128, D], BF16))
                pw = p1.enter_context(nc.psum_tensor("pw", [128, D], BF16))

                p.op("gpsimd", lambda g: g.dma_start(out=identb[:], in_=identb_h), incs=[("lw", 16)])
                p.op("gpsimd", lambda g: g.dma_start(out=identf[:], in_=identf_h), incs=[("lw", 16)])
                p.op("gpsimd", lambda g: g.dma_start(out=onesb[:], in_=onesb_h), incs=[("lw", 16)])
                p.op("gpsimd", lambda g: g.dma_start(out=onesf[:], in_=onesf_h), incs=[("lw", 16)])
                p.op("vector", lambda v: v.memset(epscol[:], LN_EPS), incs=[("dve", 1)])
                p.op("vector", lambda v: v.memset(Mk2[:], 0.0), incs=[("dve", 1)])
                ones_view = V_res[:].rearrange("p c (h e) -> p c h e", e=65)[:, :, :, 64:65]
                p.op("vector", lambda v: v.memset(ones_view, 1.0), incs=[("dve", 1)])

                with nc.sbuf_tensor("wstg", [128, 4, D], BF16) as wstg:
                    prep_weight(wk_h.rearrange("(a p) d -> p a d", p=128), 8, wkT, 8, wstg, pw)
                    prep_weight(wv_h.rearrange("(a p) d -> p a d", p=128), 8, wvT, 8, wstg, pw)
                pe_prep = p.tok("pe")
                dve_prep = p.tok("dve")

                lx = [0, 0]
                lt = [0, 0]
                spx2 = [0, 0]
                xn_tok = 0
                xnT_toks = [0, 0]
                kT_toks = [0, 0]
                sqrt_tok = [0, 0]
                rope_tok = [0, 0]
                evx_toks = [0, 0]
                mv_tok = [0, 0]
                evv_tok = 0
                evk_tok = 0
                for c in range(NCH):
                    sl = c % 2
                    xw = [("dve", xn_tok), ("act", sqrt_tok[sl])]
                    if c < 2:
                        xw += [("pe", pe_prep), ("dve", dve_prep)]
                    p.op("gpsimd", lambda g, c=c, sl=sl: g.dma_start(out=xbuf[sl][:], in_=x_h[c * 128:(c + 1) * 128, :]),
                         waits=xw, incs=[(f"lx{sl}", 16)])
                    lx[sl] += 16
                    p.op("gpsimd", lambda g, c=c, sl=sl: g.dma_start(out=tC[sl][:], in_=tabC_h[c * 128:(c + 1) * 128, :]),
                         waits=[("dve", rope_tok[sl])], incs=[(f"lt{sl}", 16)])
                    p.op("gpsimd", lambda g, c=c, sl=sl: g.dma_start(out=tS[sl][:], in_=tabS_h[c * 128:(c + 1) * 128, :]),
                         incs=[(f"lt{sl}", 16)])
                    lt[sl] += 32
                    p.op("vector", lambda v, sl=sl: v.reduce_sum(rcol[:], xbuf[sl][:], axis=AX.X),
                         waits=[(f"lx{sl}", lx[sl])])
                    p.op("vector", lambda v: v.tensor_scalar_mul(mcol[:], rcol[:], INV_D))
                    tt = p.op("vector", lambda v, sl=sl: v.tensor_scalar_sub(xbuf[sl][:], xbuf[sl][:], mcol[:, 0:1]),
                              incs=[("dve", 1)])
                    xc_tok = tt["dve"]
                    p.op("scalar", lambda s, sl=sl: s.activation(t2b[:], xbuf[sl][:], AF.Square, accum_out=vcol[:, 0:1]),
                         waits=[("dve", xc_tok)])
                    tt = p.op("scalar", lambda s: s.activation(rscol[:], vcol[:, 0:1], AF.Sqrt, bias=epscol[:, 0:1], scale=INV_D),
                              incs=[("act", 1)])
                    sqrt_tok[sl] = tt["act"]
                    p.op("vector", lambda v: v.reciprocal(rstdc[:], rscol[:]), waits=[("act", sqrt_tok[sl])])
                    tt = p.op("vector", lambda v, sl=sl: v.tensor_scalar(xnb[sl][:], xbuf[sl][:], rstdc[:, 0:1], None, ALU.mult),
                              waits=[("pe", xnT_toks[sl])], incs=[("dve", 1)])
                    xn_tok = tt["dve"]
                    tt = p.op("tensor", lambda t_, sl=sl: [
                            t_.transpose(pxnT[:, j * 128:(j + 1) * 128], xnb[sl][:, j * 128:(j + 1) * 128], identb[:])
                            for j in range(8)][-1],
                            waits=[("dve", xn_tok), ("dve", evx_toks[1 - sl])], incs=[("pe", 1)])
                    xnT_toks[sl] = tt["pe"]
                    tt = p.op("vector", lambda v, sl=sl: v.tensor_copy(xstg[sl][:], pxnT[:].rearrange("p (j t) -> p j t", t=128)),
                              waits=[("pe", xnT_toks[sl]), ("pe", mv_tok[sl]), (f"spx{sl}", spx2[sl])], incs=[("dve", 1)])
                    evx_toks[sl] = tt["dve"]
                    p.op("sync", lambda s, c=c, sl=sl: s.dma_start(out=xn_sp[c], in_=xstg[sl][:]),
                         waits=[("dve", evx_toks[sl])], incs=[(f"spx{sl}", 16)])
                    spx2[sl] += 16
                    def kv_mms(t_, sl, wT, dst):
                        last = None
                        for osl in range(2):
                            for dc in range(8):
                                last = t_.matmul(dst[:, osl * 512:(osl + 1) * 512],
                                                 xstg[sl][:, dc, :],
                                                 wT[:, dc, osl * 512:(osl + 1) * 512],
                                                 start=(dc == 0), stop=(dc == 7))
                        return last
                    tt = p.op("tensor", lambda t_, sl=sl: kv_mms(t_, sl, wkT, pk),
                              waits=[("dve", evx_toks[sl]), ("dve", rope_tok[1 - sl])], incs=[("pe", 1)])
                    mk_tok = tt["pe"]
                    tt = p.op("tensor", lambda t_, sl=sl: kv_mms(t_, sl, wvT, pv),
                              waits=[("dve", evv_tok)], incs=[("pe", 1)])
                    mv_tok[sl] = tt["pe"]
                    p.op("vector", lambda v, sl=sl: v.tensor_tensor(t1b[:], pk[:], tC[sl][:], ALU.mult),
                         waits=[("pe", mk_tok), (f"lt{sl}", lt[sl]), ("act", sqrt_tok[sl])])
                    p.op("vector", lambda v, sl=sl: v.tensor_tensor(t2b[:, 0::2], pk[:, 1::2], tS[sl][:, 0::2], ALU.mult))
                    p.op("vector", lambda v, sl=sl: v.tensor_tensor(t2b[:, 1::2], pk[:, 0::2], tS[sl][:, 1::2], ALU.mult))
                    tt = p.op("vector", lambda v, sl=sl: v.tensor_add(krb[sl][:], t1b[:], t2b[:]),
                              waits=[("pe", kT_toks[sl])], incs=[("dve", 1)])
                    rope_tok[sl] = tt["dve"]
                    p.op("vector", lambda v, sl=sl: v.tensor_tensor(t1b[:], krb[sl][:], krb[sl][:], ALU.mult))
                    p.op("vector", lambda v: v.reduce_sum(khd[:].unsqueeze(2), t1b[:].rearrange("p (h e) -> p h e", e=64), axis=AX.X))
                    p.op("vector", lambda v: v.tensor_max(Mk2[:], Mk2[:], khd[:]))
                    vdst = V_res[:].rearrange("p c (h e) -> p c h e", e=65)
                    tt = p.op("vector", lambda v, c=c, vdst=vdst: v.tensor_copy(
                            vdst[:, c, :, 0:64], pv[:].rearrange("p (h e) -> p h e", e=64)),
                            waits=[("pe", mv_tok[sl])], incs=[("dve", 1)])
                    evv_tok = tt["dve"]
                    tt = p.op("tensor", lambda t_, sl=sl: [
                            t_.transpose(pkT[:, j * 128:(j + 1) * 128], krb[sl][:, j * 128:(j + 1) * 128], identb[:])
                            for j in range(8)][-1],
                            waits=[("dve", rope_tok[sl]), ("dve", evk_tok)], incs=[("pe", 1)])
                    kT_toks[sl] = tt["pe"]
                    tt = p.op("vector", lambda v, c=c: v.tensor_copy(
                            K_res[:][:, :, c * 128:(c + 1) * 128],
                            pkT[:].rearrange("p (j t) -> p j t", t=128)),
                            waits=[("pe", tt["pe"])], incs=[("dve", 1)])
                    evk_tok = tt["dve"]

                pmk = pw[:].bitcast(F32)[0:16, 0:128]
                tt = p.op("tensor", lambda t_: t_.transpose(pmk, Mk2[:, 0:16], identf[:]),
                          waits=[("dve", p.tok("dve"))], incs=[("pe", 1)])
                tt = p.op("vector", lambda v: v.reduce_max(mk2c[:], pmk, axis=AX.X),
                          waits=[("pe", tt["pe"])], incs=[("dve", 1)])
                p.op("vector", lambda v: v.tensor_copy(Mkcol[:], mk2c[:]), incs=[("dve", 1)])
                if "kv" in dbg:
                    p.op("sync", lambda s: s.dma_start(out=k_dbg, in_=K_res[:]),
                         waits=[("dve", p.tok("dve"))], incs=[("out0", 16)])
                    p.op("sync", lambda s: s.dma_start(out=v_dbg, in_=V_res[:]),
                         incs=[("out0", 16)])
                if "wt" in dbg:
                    p.op("sync", lambda s: s.dma_start(out=wkT_dbg, in_=wkT[:]),
                         waits=[("dve", p.tok("dve"))], incs=[("out0", 16)])
                    p.op("sync", lambda s: s.dma_start(out=wvT_dbg, in_=wvT[:]),
                         incs=[("out0", 16)])
                p.barrier()
                p.emit_phase(nc)

            with contextlib.ExitStack() as p2:
                wqT = p2.enter_context(nc.sbuf_tensor("wqT", [128, 8, D], BF16))
                maskt = p2.enter_context(nc.sbuf_tensor("maskt", [128, 4, 512], BF16))
                xnq = [p2.enter_context(nc.sbuf_tensor(f"xnq{i}", [128, 8, 128], BF16)) for i in range(2)]
                tC2 = [p2.enter_context(nc.sbuf_tensor(f"tC2{i}", [128, D], BF16)) for i in range(2)]
                tS2 = [p2.enter_context(nc.sbuf_tensor(f"tS2{i}", [128, D], BF16)) for i in range(2)]
                qrb = p2.enter_context(nc.sbuf_tensor("qrb", [128, D], BF16))
                Qfm = p2.enter_context(nc.sbuf_tensor("Qfm", [128, 8, 512], BF16))
                t1q = p2.enter_context(nc.sbuf_tensor("t1q", [128, D], F32))
                t2q = p2.enter_context(nc.sbuf_tensor("t2q", [128, D], F32))
                qhd = p2.enter_context(nc.sbuf_tensor("qhd", [128, 16], F32))
                Mq2 = p2.enter_context(nc.sbuf_tensor("Mq2", [128, 16], F32))
                mq2c = p2.enter_context(nc.sbuf_tensor("mq2c", [16, 1], F32))
                mqc = p2.enter_context(nc.sbuf_tensor("mqc", [16, 1], F32))
                b16 = p2.enter_context(nc.sbuf_tensor("b16", [16, 1], F32))
                bnr = p2.enter_context(nc.sbuf_tensor("bnr", [16, 1], I32))
                btb = p2.enter_context(nc.sbuf_tensor("btb", [16, 1], F32))
                brow = p2.enter_context(nc.sbuf_tensor("brow", [1, 16], F32))
                bpair = p2.enter_context(nc.sbuf_tensor("bpair", [1, 8], F32))
                bcols = p2.enter_context(nc.sbuf_tensor("bcols", [128, 8], F32))
                ptb = [p2.enter_context(nc.sbuf_tensor(f"ptb{i}", [128, 1024], BF16)) for i in range(3)]
                recL = [p2.enter_context(nc.sbuf_tensor(f"recL{i}", [1, 512], BF16)) for i in range(2)]
                onesr = p2.enter_context(nc.sbuf_tensor("onesr", [1, 512], F32))
                recLf = p2.enter_context(nc.sbuf_tensor("recLf", [1, 512], F32))
                Osb = p2.enter_context(nc.sbuf_tensor("Osb", [64, 512], F32))
                Ostg = p2.enter_context(nc.sbuf_tensor("Ostg", [128, 8, 512], BF16))

                pS = [p2.enter_context(nc.psum_tensor(f"pS{i}", [128, 1024], F32)) for i in range(2)]
                pO = [[p2.enter_context(nc.psum_tensor(f"pO{i}{hh}", [128, 512], F32)) for hh in range(2)] for i in range(2)]
                pq0, pq1 = pO[0][0], pO[0][1]

                p.op("gpsimd", lambda g: g.dma_start(out=maskt[:], in_=masks_h.rearrange("r p n -> p r n")),
                     incs=[("lw", 16)])
                p.op("vector", lambda v: v.memset(onesr[:], 1.0), incs=[("dve", 1)])
                pwq = pq0[:].bitcast(BF16)[:, 0:1024]
                with nc.sbuf_tensor("wstg2", [128, 2, D], BF16) as wstg2:
                    prep_weight(wq_h.rearrange("(a p) d -> p a d", p=128), 8, wqT, 8, wstg2, pwq)
                pe_prep2 = p.tok("pe")
                dve_prep2 = p.tok("dve")

                lxn = 0
                lq = [0, 0]
                spo = 0
                rope2_tok = [0, 0]
                qT_tok = 0
                evq_tok = 0
                mmq_last = 0
                oev_act = 0
                oev_dve = 0
                oev_set = [(0, 0), (0, 0)]
                oev_mult = 0
                lb_tok = 0
                for j in range(NQT):
                    for ts in range(4):
                        sl = ts % 2
                        cidx = 4 * j + ts
                        qsl = cidx % 2
                        if not (ts == 0 and j > 0):
                            xw = [("pe", mmq_last), ("spx0", 256), ("spx1", 256)]
                            if j == 0 and ts == 0:
                                xw += [("pe", pe_prep2), ("dve", dve_prep2)]
                            p.op("gpsimd", lambda g, cidx=cidx, qsl=qsl: g.dma_start(out=xnq[qsl][:], in_=xn_sp[cidx]),
                                 waits=xw, incs=[("lxn", 16)])
                            lxn += 16
                        if not (ts == 0 and j > 0):
                            p.op("gpsimd", lambda g, cidx=cidx, sl=sl: g.dma_start(out=tC2[sl][:], in_=tabC_h[cidx * 128:(cidx + 1) * 128, :]),
                                 waits=[("dve", rope2_tok[sl])], incs=[(f"lq{sl}", 16)])
                            p.op("gpsimd", lambda g, cidx=cidx, sl=sl: g.dma_start(out=tS2[sl][:], in_=tabS_h[cidx * 128:(cidx + 1) * 128, :]),
                                 incs=[(f"lq{sl}", 16)])
                            lq[sl] += 32
                        waits = [("lxn", lxn), ("dve", evq_tok), ("dve", rope2_tok[1 - sl])]
                        if ts == 0:
                            waits += [("dve", oev_dve)]
                        def q_mms(t_, qsl=qsl):
                            last = None
                            for osl, dst in ((0, pq0), (1, pq1)):
                                for dc in range(8):
                                    last = t_.matmul(dst[:],
                                                     xnq[qsl][:, dc, :],
                                                     wqT[:, dc, osl * 512:(osl + 1) * 512],
                                                     start=(dc == 0), stop=(dc == 7))
                            return last
                        tt = p.op("tensor", q_mms, waits=waits, incs=[("pe", 1)])
                        mmq_last = tt["pe"]
                        for half, psrc in ((0, pq0), (1, pq1)):
                            h0 = half * 512
                            p.op("vector", lambda v, sl=sl, psrc=psrc, h0=h0: v.tensor_tensor(
                                    t1q[:, h0:h0 + 512], psrc[:], tC2[sl][:, h0:h0 + 512], ALU.mult),
                                 waits=([("pe", mmq_last), (f"lq{sl}", lq[sl])] if half == 0 else []))
                            p.op("vector", lambda v, sl=sl, psrc=psrc, h0=h0: v.tensor_tensor(
                                    t2q[:, h0:h0 + 512][:, 0::2], psrc[:, 1::2], tS2[sl][:, h0:h0 + 512][:, 0::2], ALU.mult))
                            p.op("vector", lambda v, sl=sl, psrc=psrc, h0=h0: v.tensor_tensor(
                                    t2q[:, h0:h0 + 512][:, 1::2], psrc[:, 0::2], tS2[sl][:, h0:h0 + 512][:, 1::2], ALU.mult))
                        tt = p.op("vector", lambda v: v.tensor_add(qrb[:], t1q[:], t2q[:]),
                                  waits=[("pe", qT_tok)], incs=[("dve", 1)])
                        rope2_tok[sl] = tt["dve"]
                        p.op("vector", lambda v: v.tensor_tensor(t1q[:], qrb[:], qrb[:], ALU.mult))
                        p.op("vector", lambda v: v.reduce_sum(qhd[:].unsqueeze(2), t1q[:].rearrange("p (h e) -> p h e", e=64), axis=AX.X))
                        if ts == 0:
                            p.op("vector", lambda v: v.tensor_copy(Mq2[:], qhd[:]))
                        else:
                            p.op("vector", lambda v: v.tensor_max(Mq2[:], Mq2[:], qhd[:]))
                        tt = p.op("tensor", lambda t_: [
                                t_.transpose(pwq[:, jj * 128:(jj + 1) * 128], qrb[:, jj * 128:(jj + 1) * 128], identb[:])
                                for jj in range(8)][-1],
                                waits=[("dve", rope2_tok[sl])], incs=[("pe", 1)])
                        qT_tok = tt["pe"]
                        tt = p.op("vector", lambda v, ts=ts: v.tensor_copy(
                                Qfm[:, :, ts * 128:(ts + 1) * 128],
                                pwq[:].rearrange("p (jj t) -> p jj t", t=128)),
                                waits=[("pe", tt["pe"])], incs=[("dve", 1)])
                        evq_tok = tt["dve"]

                    tt = p.op("tensor", lambda t_: t_.transpose(pq0[:].bitcast(F32)[0:16, 0:128], Mq2[:, 0:16], identf[:]),
                              waits=[("dve", p.tok("dve"))], incs=[("pe", 1)])
                    tt = p.op("vector", lambda v: v.reduce_max(mq2c[:], pq0[0:16, 0:128], axis=AX.X),
                              waits=[("pe", tt["pe"])], incs=[("dve", 1)])
                    p.op("vector", lambda v: v.tensor_mul(b16[:], mq2c[:], Mkcol[:]))
                    ynr = bnr[:].bitcast(F32)
                    p.op("vector", lambda v: v.tensor_scalar(bnr[:], b16[:].bitcast(I32), 1, None, ALU.arith_shift_right))
                    p.op("vector", lambda v: v.tensor_scalar(bnr[:], bnr[:], -1, 0x5F3759DF, ALU.mult, ALU.add))
                    for _nr in range(2):
                        p.op("vector", lambda v: v.tensor_mul(btb[:], b16[:], ynr))
                        p.op("vector", lambda v: v.tensor_mul(btb[:], btb[:], ynr))
                        p.op("vector", lambda v: v.tensor_scalar(btb[:], btb[:], -0.5, 1.5, ALU.mult, ALU.add))
                        p.op("vector", lambda v: v.tensor_mul(ynr, ynr, btb[:]))
                    p.op("vector", lambda v: v.tensor_mul(b16[:], b16[:], ynr))
                    tt = p.op("vector", lambda v: v.tensor_scalar(b16[:], b16[:], -1.01, -0.5, ALU.mult, ALU.add),
                              incs=[("dve", 1)])
                    tt = p.op("tensor", lambda t_: t_.transpose(pq0[:].bitcast(F32)[0:1, 0:16], b16[:], identf[0:16, 0:16]),
                              waits=[("dve", tt["dve"])], incs=[("pe", 1)])
                    p.op("vector", lambda v: v.tensor_copy(brow[:], pq0[0:1, 0:16]),
                         waits=[("pe", tt["pe"])])
                    tt = p.op("vector", lambda v: v.tensor_tensor(bpair[:], brow[:, 0::2], brow[:, 1::2], ALU.min),
                              incs=[("dve", 1)])
                    tt = p.op("tensor", lambda t_: t_.matmul(pq0[:, 0:8], onesf[:], bpair[:], start=True, stop=True),
                              waits=[("dve", tt["dve"])], incs=[("pe", 1)])
                    tt = p.op("vector", lambda v: v.tensor_copy(bcols[:], pq0[:, 0:8]),
                              waits=[("pe", tt["pe"])], incs=[("dve", 1)])
                    bc_tok = tt["dve"]

                    nch_j = 4 * (j + 1)
                    insts = [(pr, c) for pr in range(8) for c in range(nch_j)]
                    exp_toks = {}
                    av_toks = {}
                    mask_toks = {}
                    evac_jobs = {}

                    def emit_s_exp(i):
                        pr, c = insts[i]
                        sl2 = i % 2
                        sl3 = i % 3
                        dr = c - 4 * j
                        def s_mms(t_, pr=pr, c=c, sl2=sl2):
                            t_.matmul(pS[sl2][:, 0:512], K_res[0:64, pr, c * 128:(c + 1) * 128],
                                      Qfm[0:64, pr, :], start=True, stop=True)
                            return t_.matmul(pS[sl2][:, 512:1024], K_res[64:128, pr, c * 128:(c + 1) * 128],
                                             Qfm[64:128, pr, :], start=True, stop=True,
                                             tile_position=(64, 0))
                        sw = [("act", exp_toks.get(i - 2, 0))]
                        if i < 4:
                            sw.append(("dve", bc_tok))
                        tt = p.op("tensor", s_mms, waits=sw, incs=[("pe", 1)])
                        tt = p.op("scalar", lambda s, pr=pr, sl3=sl3, sl2=sl2: s.activation(
                                ptb[sl3][:], pS[sl2][:], AF.Exp, bias=bcols[:, pr:pr + 1]),
                                waits=[("pe", tt["pe"]), ("pe", av_toks.get(i - 3, 0))],
                                incs=[("act", 1)], selfsync=False)
                        exp_toks[i] = tt["act"]
                        if dr >= 0:
                            p.op("vector", lambda v, dr=dr, sl3=sl3: v.tensor_tensor(
                                    ptb[sl3][:, 0:512], ptb[sl3][:, 0:512], maskt[:, dr, :], ALU.mult),
                                    waits=[("act", exp_toks[i])])
                            tt = p.op("vector", lambda v, dr=dr, sl3=sl3: v.tensor_tensor(
                                    ptb[sl3][:, 512:1024], ptb[sl3][:, 512:1024], maskt[:, dr, :], ALU.mult),
                                    incs=[("dve", 1)])
                            mask_toks[i] = tt["dve"]

                    def emit_av(i):
                        nonlocal lb_tok
                        pr, c = insts[i]
                        sl2 = i % 2
                        st2 = pr % 2
                        waits = [("act", exp_toks[i])]
                        if i in mask_toks:
                            waits.append(("dve", mask_toks[i]))
                        if c == 0:
                            waits += [("dve", oev_set[st2][0])]
                        sl3 = i % 3
                        def av_mms(t_, pr=pr, c=c, sl3=sl3, st2=st2, first=(c == 0), last=(c == nch_j - 1)):
                            t_.matmul(pO[st2][0][0:65, :], V_res[:, c, 130 * pr:130 * pr + 65],
                                      ptb[sl3][:, 0:512], start=first, stop=last)
                            return t_.matmul(pO[st2][1][0:65, :], V_res[:, c, 130 * pr + 65:130 * pr + 130],
                                             ptb[sl3][:, 512:1024], start=first, stop=last)
                        tt = p.op("tensor", av_mms, waits=waits, incs=[("pe", 1)])
                        av_toks[i] = tt["pe"]
                        if c == nch_j - 1:
                            rec = emit_evac_early(pr, av_toks[i])
                            pending.append((i, pr, av_toks[i], rec))

                    def emit_evac_early(pr, av_last):
                        nonlocal lb_tok
                        rec = []
                        for hh in range(2):
                            bank = pO[pr % 2][hh]
                            p.op("vector", lambda v, bank=bank: v.reciprocal(recLf[:], bank[64:65, :]),
                                 waits=[("pe", av_last)])
                            tt = p.op("vector", lambda v, hh=hh: v.tensor_copy(recL[hh][:], recLf[:]),
                                      waits=[("pe", lb_tok)], incs=[("dve", 1)])
                            rec.append(tt["dve"])
                        return rec

                    def emit_evac_late(pr, av_last, rec):
                        nonlocal lb_tok, oev_mult
                        st2 = pr % 2
                        dv = 0
                        for hh in range(2):
                            bank = pO[st2][hh]
                            tt = p.op("vector", lambda v, bank=bank: v.tensor_copy(Osb[:], bank[0:64, :]),
                                      waits=[("dve", oev_mult)], incs=[("dve", 1)])
                            ac = tt["dve"]
                            tt = p.op("tensor", lambda t_, bank=bank, hh=hh: t_.matmul(bank[0:64, :], onesb[:], recL[hh][:], start=True, stop=True),
                                      waits=[("dve", max(rec[hh], ac))], incs=[("pe", 1)])
                            lb_tok = tt["pe"]
                            tt = p.op("vector", lambda v, hh=hh, pr=pr, bank=bank: v.tensor_tensor(
                                    Ostg[hh * 64:(hh + 1) * 64, pr, :], Osb[:], bank[0:64, :], ALU.mult),
                                    waits=[("pe", lb_tok), ("spo0", spo)],
                                    incs=[("dve", 1)])
                            dv = tt["dve"]
                            oev_mult = dv
                        oev_set[st2] = (dv, dv)

                    pending = []
                    n_inst = len(insts)
                    emit_s_exp(0)
                    if n_inst > 1:
                        emit_s_exp(1)
                    pf_i = max(2, (6 * n_inst) // 10)
                    for i in range(2, n_inst):
                        emit_s_exp(i)
                        while pending and pending[0][0] <= i - 3:
                            job = pending.pop(0)
                            emit_evac_late(job[1], job[2], job[3])
                        emit_av(i - 2)
                        if i == pf_i and j + 1 < NQT:
                            cnext = 4 * (j + 1)
                            p.op("sync", lambda g, cnext=cnext: g.dma_start(out=xnq[cnext % 2][:], in_=xn_sp[cnext]),
                                 waits=[("pe", mmq_last)], incs=[("lxn", 16)])
                            lxn += 16
                            p.op("gpsimd", lambda g, cnext=cnext: g.dma_start(out=tC2[0][:], in_=tabC_h[cnext * 128:(cnext + 1) * 128, :]),
                                 waits=[("dve", rope2_tok[0])], incs=[("lq0", 16)])
                            p.op("gpsimd", lambda g, cnext=cnext: g.dma_start(out=tS2[0][:], in_=tabS_h[cnext * 128:(cnext + 1) * 128, :]),
                                 incs=[("lq0", 16)])
                            lq[0] += 32
                    if n_inst > 1:
                        emit_av(n_inst - 2)
                    emit_av(n_inst - 1)
                    for job in pending:
                        emit_evac_late(job[1], job[2], job[3])
                    pending = []
                    pending = []
                    oev_dve = max(oev_set[0][0], oev_set[1][0])
                    oev_act = oev_dve
                    for cc in range(4):
                        p.op("sync", lambda s, j=j, cc=cc: s.dma_start(
                                out=o_sp[4 * j + cc], in_=Ostg[:, :, cc * 128:(cc + 1) * 128]),
                             waits=[("dve", oev_dve)], incs=[("spo0", 16)])
                    spo += 64
                p.barrier()
                p.emit_phase(nc)

        with contextlib.ExitStack() as p3:
            wgT = p3.enter_context(nc.sbuf_tensor("wgT", [128, 8, HID], BF16))
            wuT = p3.enter_context(nc.sbuf_tensor("wuT", [128, 8, HID], BF16))
            woT = p3.enter_context(nc.sbuf_tensor("woT", [128, 8, D], BF16))
            wdT = p3.enter_context(nc.sbuf_tensor("wdT", [128, 24, D], BF16))

            with nc.sbuf_tensor("wstg3", [128, 16, D], BF16) as wstg3, \
                 nc.psum_tensor("pw3", [128, D], BF16) as pw3:
                prep_weight(wg_h.rearrange("(a p) d -> p a d", p=128), 24, wgT, 8, wstg3, pw3)
                prep_weight(wu_h.rearrange("(a p) d -> p a d", p=128), 24, wuT, 8, wstg3, pw3)
                prep_weight(wo_h.rearrange("(a p) d -> p a d", p=128), 8, woT, 8, wstg3, pw3)
            with nc.sbuf_tensor("wstg4", [128, 2, HID], BF16) as wstg4, \
                 nc.psum_tensor("pw4", [128, D], BF16) as pw4:
                srcq = wd_h.rearrange("(a p) h -> p a h", p=128)
                for q in range(4):
                    p.op("gpsimd", lambda g, q=q: g.dma_start(out=wstg4[:], in_=srcq[:, 2 * q:2 * q + 2, :]),
                         waits=[("pe", p.tok("pe")), ("lw", p.tok("lw"))], incs=[("lw", 16)])
                    lwtok = p.tok("lw")
                    for a in range(2):
                        dc = 2 * q + a
                        for b in range(3):
                            for hc in range(8):
                                hcg = b * 8 + hc
                                p.op("tensor", lambda t_, a=a, hcg=hcg, hc=hc: t_.transpose(
                                        pw4[:, hc * 128:(hc + 1) * 128],
                                        wstg4[:, a, hcg * 128:(hcg + 1) * 128], identb[:]),
                                     waits=([("lw", lwtok), ("dve", p.tok("dve"))] if hc == 0 else []),
                                     incs=([("pe", 1)] if hc == 7 else []))
                            wdst = wdT[:][:, b * 8:(b + 1) * 8, dc * 128:(dc + 1) * 128]
                            p.op("vector", lambda v, wdst=wdst: v.tensor_copy(
                                    wdst, pw4[:].rearrange("p (hh t) -> p hh t", t=128)),
                                 waits=[("pe", p.tok("pe"))], incs=[("dve", 1)])
            pe_prep3 = p.tok("pe")
            dve_prep3 = p.tok("dve")

            xn3 = [p3.enter_context(nc.sbuf_tensor(f"xn3{i}", [128, 8, 128], BF16)) for i in range(2)]
            O3 = [p3.enter_context(nc.sbuf_tensor(f"O3{i}", [128, 8, 128], BF16)) for i in range(2)]
            x3 = p3.enter_context(nc.sbuf_tensor("x3", [128, D], F32))
            htm = p3.enter_context(nc.sbuf_tensor("htm", [128, HID], BF16))
            hfm = p3.enter_context(nc.sbuf_tensor("hfm", [128, 24, 128], BF16))
            gsil = p3.enter_context(nc.sbuf_tensor("gsil", [128, 512], F32))
            gdn = p3.enter_context(nc.sbuf_tensor("gdn", [128, 512], F32))
            sgf = p3.enter_context(nc.sbuf_tensor("sgf", [128, 512], F32))
            outb = [p3.enter_context(nc.sbuf_tensor(f"outb{i}", [128, D], F32)) for i in range(2)]

            pA = p3.enter_context(nc.psum_tensor("pA", [128, 512], F32))
            pB = p3.enter_context(nc.psum_tensor("pB", [128, 512], F32))
            pC = p3.enter_context(nc.psum_tensor("pC", [128, 512], F32))
            pD = p3.enter_context(nc.psum_tensor("pD", [128, 512], F32))
            phT = p3.enter_context(nc.psum_tensor("phT", [128, 1024], BF16))
            pd0 = p3.enter_context(nc.psum_tensor("pd0", [128, 512], F32))
            pd1 = p3.enter_context(nc.psum_tensor("pd1", [128, 512], F32))
            psj0 = p3.enter_context(nc.psum_tensor("psj0", [128, 512], F32))

            l3a = [0, 0]
            l3b = [0, 0]
            mmgu_tok = {}
            silu_tok = {}
            hmul_tok = {}
            sg_tok = {}
            ep_tok = [0, 0]
            sj_tok = [0, 0]
            dn_tok = [0, 0]
            he_tok = 0
            outc = [0, 0]
            for ti in range(NCH):
                sl = ti % 2
                xw = [("pe", mmgu_tok.get((ti - 2, 5), 0))]
                ow = [("pe", sj_tok[sl]), ("spo0", 512)]
                if ti < 2:
                    xw += [("pe", pe_prep3), ("dve", dve_prep3)]
                    ow += [("pe", pe_prep3), ("dve", dve_prep3)]
                p.op("gpsimd", lambda g, ti=ti, sl=sl: g.dma_start(out=xn3[sl][:], in_=xn_sp[ti]),
                     waits=xw, incs=[(f"l3a{sl}", 16)])
                p.op("gpsimd", lambda g, ti=ti, sl=sl: g.dma_start(out=O3[sl][:], in_=o_sp[ti]),
                     waits=ow, incs=[(f"l3a{sl}", 16)])
                l3a[sl] += 32
                xw2 = [("dve", sg_tok.get((ti - 1, 1), 0))]
                if ti < 2:
                    xw2 += [("dve", dve_prep3)]
                p.op("sync", lambda s, ti=ti: s.dma_start(out=x3[:], in_=x_h[ti * 128:(ti + 1) * 128, :]),
                     waits=xw2, incs=[("l3b0", 16)])
                l3b[sl] += 16
                hT_sched = {3: 0, 4: 1, 5: 2}
                for hs in range(6):
                    pg = (pA, pB)[hs % 2]
                    pu = (pC, pD)[hs % 2]
                    if hs >= 2:
                        frees = [("act", silu_tok.get((ti, hs - 2), 0)), ("dve", hmul_tok.get((ti, hs - 2), 0))]
                    else:
                        frees = [("act", silu_tok.get((ti - 1, hs + 4), 0)),
                                 ("dve", hmul_tok.get((ti - 1, hs + 4), 0))]
                    def gu_mms(t_, sl=sl, hs=hs, pg=pg, pu=pu):
                        last = None
                        for wT, dst in ((wgT, pg), (wuT, pu)):
                            for dc in range(8):
                                last = t_.matmul(dst[:], xn3[sl][:, dc, :],
                                                 wT[:, dc, hs * 512:(hs + 1) * 512],
                                                 start=(dc == 0), stop=(dc == 7))
                        return last
                    tt = p.op("tensor", gu_mms, waits=[(f"l3a{sl}", l3a[sl])] + frees, incs=[("pe", 1)])
                    mmgu_tok[(ti, hs)] = tt["pe"]
                    tt = p.op("scalar", lambda s, pg=pg: s.activation(gsil[:], pg[:], AF.Silu, scale=FFN_S),
                              waits=[("pe", tt["pe"]),
                                     ("dve", hmul_tok.get((ti, hs - 1), 0) if hs >= 1 else hmul_tok.get((ti - 1, 5), 0))],
                              incs=[("act", 1)])
                    silu_tok[(ti, hs)] = tt["act"]
                    tt = p.op("vector", lambda v, hs=hs, pu=pu: v.tensor_tensor(
                            htm[:, hs * 512:(hs + 1) * 512], gsil[:], pu[:], ALU.mult),
                            waits=[("act", tt["act"])], incs=[("dve", 1)])
                    hmul_tok[(ti, hs)] = tt["dve"]
                for b in range(3):
                    tw = [("dve", hmul_tok[(ti, 2 * b + 1)]), ("dve", he_tok)]
                    if b == 0:
                        tw.append(("dve", sg_tok.get((ti - 1, 1), 0)))
                    tt = p.op("tensor", lambda t_, b=b: [
                            t_.transpose(phT[:, k * 128:(k + 1) * 128],
                                         htm[:, (b * 8 + k) * 128:(b * 8 + k + 1) * 128], identb[:])
                            for k in range(8)][-1],
                            waits=tw,
                            incs=[("pe", 1)])
                    tt = p.op("vector", lambda v, b=b: v.tensor_copy(
                            hfm[:, b * 8:(b + 1) * 8, :], phT[:].rearrange("p (k t) -> p k t", t=128)),
                            waits=[("pe", tt["pe"])], incs=[("dve", 1)])
                    he_tok = tt["dve"]
                for ds in range(2):
                    pd = (pd0, pd1)[ds]
                    psj = psj0 if ds == 0 else phT[:].bitcast(F32)[:, 0:512]
                    def dn_mms(t_, ds=ds, pd=pd):
                        last = None
                        for hc in range(24):
                            last = t_.matmul(pd[:], hfm[:, hc, :], wdT[:, hc, ds * 512:(ds + 1) * 512],
                                             start=(hc == 0), stop=(hc == 23))
                        return last
                    tt = p.op("tensor", dn_mms, waits=[("dve", he_tok), ("act", dn_tok[ds])], incs=[("pe", 1)])
                    pe_dn = tt["pe"]
                    def sj_mms(t_, ds=ds, psj=psj, sl=sl):
                        last = None
                        for oc in range(8):
                            last = t_.matmul(psj[:] if ds == 0 else psj, O3[sl][:, oc, :], woT[:, oc, ds * 512:(ds + 1) * 512],
                                             start=(oc == 0), stop=(oc == 7))
                        return last
                    sjw = [(f"l3a{sl}", l3a[sl])]
                    if ds == 0:
                        sjw.append(("dve", sg_tok.get((ti - 1, 0), 0)))
                    else:
                        sjw.append(("dve", he_tok))
                    tt = p.op("tensor", sj_mms, waits=sjw, incs=[("pe", 1)])
                    sj_tok[sl] = tt["pe"]
                    gprev = sg_tok.get((ti, 0), 0) if ds == 1 else sg_tok.get((ti - 1, 1), 0)
                    tt = p.op("scalar", lambda s, pd=pd: s.activation(gdn[:], pd[:], AF.Silu, scale=FFN_S2),
                              waits=[("pe", pe_dn), ("dve", gprev)], incs=[("act", 1)])
                    dn_tok[ds] = tt["act"]
                    p.op("vector", lambda v, psj=psj: v.tensor_tensor(sgf[:], gdn[:], psj[:], ALU.mult),
                         waits=[("act", dn_tok[ds]), ("pe", sj_tok[sl]), ("l3b0", sum(l3b)),
                                (f"out{sl}", outc[sl])])
                    tt = p.op("vector", lambda v, ds=ds, sl=sl: v.tensor_add(
                            outb[sl][:, ds * 512:(ds + 1) * 512], sgf[:], x3[:, ds * 512:(ds + 1) * 512]),
                            incs=[("dve", 1)])
                    sg_tok[(ti, ds)] = tt["dve"]
                    ep_tok[sl] = tt["dve"]
                p.op("sync", lambda s, ti=ti, sl=sl: s.dma_start(out=out_h[ti * 128:(ti + 1) * 128, :], in_=outb[sl][:]),
                     waits=[("dve", ep_tok[sl])], incs=[(f"out{sl}", 16)])
                outc[sl] += 16
            p.op("sync", lambda s: s.nop(), waits=[("out0", p.tok("out0")), ("out1", p.tok("out1"))])
            p.emit_phase(nc)

    return nc


_NC_CACHE = {}


def _get_nc(dbg=()):
    key = tuple(sorted(dbg))
    if key not in _NC_CACHE:
        _NC_CACHE[key] = build_nc(dbg)
    return _NC_CACHE[key]


def kernel(L0_attn_x, L0_attn_wq, L0_attn_wk, L0_attn_wv, L0_attn_wo,
           L0_ffn_w_gate, L0_ffn_w_up, L0_ffn_w_down):
    consts = _host_consts()
    nc = _get_nc()
    x = np.ascontiguousarray(np.asarray(L0_attn_x, dtype=np.float32))
    shared = dict(
        wq=np.ascontiguousarray(np.asarray(L0_attn_wq, np.float32)),
        wk=np.ascontiguousarray(np.asarray(L0_attn_wk, np.float32)),
        wv=np.ascontiguousarray(np.asarray(L0_attn_wv, np.float32)),
        wo=np.ascontiguousarray(np.asarray(L0_attn_wo, np.float32)),
        wg=np.ascontiguousarray(np.asarray(L0_ffn_w_gate, np.int32)),
        wu=np.ascontiguousarray(np.asarray(L0_ffn_w_up, np.int32)),
        wd=np.ascontiguousarray(np.asarray(L0_ffn_w_down, np.int32)),
        tabC=consts["tabC"], tabS=consts["tabS"], masks=consts["masks"],
        identb_h=consts["identb"], identf_h=consts["identf"],
        onesb_h=consts["onesb"], onesf_h=consts["onesf"],
    )
    in_maps = [dict(x=x[b], **shared) for b in range(8)]
    res = run_bass_kernel_spmd(nc, in_maps, core_ids=list(range(8)))
    return np.stack([r["out"] for r in res.results], axis=0)


# version 13
# speedup vs baseline: 1.1909x; 1.1616x over previous
import contextlib
import numpy as np
import ml_dtypes
import concourse.bass as bass
import concourse.mybir as mybir
from concourse.bass_utils import run_bass_kernel_spmd

F32 = mybir.dt.float32
BF16 = mybir.dt.bfloat16
I32 = mybir.dt.int32
AF = mybir.ActivationFunctionType
ALU = mybir.AluOpType
AX = mybir.AxisListType

T, D, H, HD, HID = 4096, 1024, 16, 64, 3072
NCH = T // 128
NQT = T // 512
QK_SCALE = float(HD) ** -0.25
FFN_S = float(np.float32(1.0) / np.float32(127.0))
FFN_S2 = float(np.float32(FFN_S) * np.float32(FFN_S))
INV_D = 1.0 / D
LN_EPS = 1e-5

_CONSTS = None


def _host_consts():
    global _CONSTS
    if _CONSTS is not None:
        return _CONSTS
    import jax
    import jax.numpy as jnp
    with jax.default_device(jax.devices("cpu")[0]):
        theta = jnp.exp(jnp.log(10000.0) * jnp.arange(HD // 2, dtype=jnp.float32) / HD)
        args = jnp.arange(T, dtype=jnp.float32)[:, None] * theta
        c = np.asarray(jnp.cos(args), dtype=np.float32)
        s = np.asarray(jnp.sin(args), dtype=np.float32)
    C64 = np.empty((T, HD), np.float32)
    C64[:, 0::2] = c
    C64[:, 1::2] = c
    Ss64 = np.empty((T, HD), np.float32)
    Ss64[:, 0::2] = -s
    Ss64[:, 1::2] = s
    tabC = np.tile(C64 * QK_SCALE, (1, H)).astype(ml_dtypes.bfloat16)
    tabS = np.tile(Ss64 * QK_SCALE, (1, H)).astype(ml_dtypes.bfloat16)
    pidx = np.arange(128)[:, None]
    fidx = np.arange(512)[None, :]
    masks = np.stack([(fidx >= (128 * r + pidx)) for r in range(4)]).astype(np.float32)
    masks = masks.astype(ml_dtypes.bfloat16)
    _CONSTS = dict(
        tabC=tabC, tabS=tabS, masks=masks,
        identb=np.eye(128, dtype=ml_dtypes.bfloat16),
        identf=np.eye(128, dtype=np.float32),
        onesb=np.ones((1, 64), dtype=ml_dtypes.bfloat16),
        onesf=np.ones((1, 128), dtype=np.float32),
    )
    return _CONSTS


class Plan:

    def __init__(self, sems):
        self.ops = []
        self.cnt = {}
        self.sems = sems
        self.lo = 0

    def tok(self, sem):
        return self.cnt.get(sem, 0)

    PARANOID = False
    SELFSYNC = True

    def op(self, eng, fn, waits=(), incs=(), selfsync=True):
        if Plan.PARANOID:
            waits = list(waits) + [(s, self.tok(s)) for s in self.cnt if s not in [i[0] for i in incs]]
        if Plan.SELFSYNC and selfsync and eng in ("vector", "scalar"):
            sem = "dve" if eng == "vector" else "act"
            waits = list(waits) + [(sem, self.tok(sem))]
            if not any(s == sem for s, _ in incs):
                incs = list(incs) + [(sem, 1)]
        waits = [(s, v) for (s, v) in waits if v > 0]
        for s, amt in incs:
            self.cnt[s] = self.cnt.get(s, 0) + amt
        self.ops.append((eng, fn, waits, list(incs)))
        return {s: self.cnt[s] for s, _ in incs}

    def barrier(self):
        base = self.tok("ph")
        for e in ("sync", "gpsimd", "vector", "scalar", "tensor"):
            self.op(e, lambda eng: eng.nop(), incs=[("ph", 1)], selfsync=False)
        for e in ("sync", "gpsimd", "vector", "scalar", "tensor"):
            self.op(e, lambda eng: eng.nop(), waits=[("ph", base + 5)], selfsync=False)

    def emit_phase(self, nc):
        ops = self.ops[self.lo:]
        self.lo = len(self.ops)
        with nc.Block() as block:
            def section(eng_name):
                def run(eng):
                    for e, fn, waits, incs in ops:
                        if e != eng_name:
                            continue
                        for s, v in waits:
                            eng.wait_ge(self.sems[s], v)
                        instr = fn(eng)
                        for s, amt in incs:
                            instr.then_inc(self.sems[s], amt)
                return run
            block.sync(section("sync"))
            block.gpsimd(section("gpsimd"))
            block.vector(section("vector"))
            block.scalar(section("scalar"))
            block.tensor(section("tensor"))


SEM_NAMES = (
    "pe", "dve", "act", "pool", "ph",
    "lx0", "lx1", "lt0", "lt1", "spx0", "spx1", "lw",
    "lxn", "lq0", "lq1", "spo0", "spo1",
    "l3a0", "l3a1", "l3b0", "l3b1", "out0", "out1",
)


def build_nc(dbg=()):
    nc = bass.Bass("TRN2", target_bir_lowering=False, debug=False,
                   detect_race_conditions=False)

    x_h = nc.dram_tensor("x", [T, D], F32, kind="ExternalInput").ap()
    wq_h = nc.dram_tensor("wq", [D, D], F32, kind="ExternalInput").ap()
    wk_h = nc.dram_tensor("wk", [D, D], F32, kind="ExternalInput").ap()
    wv_h = nc.dram_tensor("wv", [D, D], F32, kind="ExternalInput").ap()
    wo_h = nc.dram_tensor("wo", [D, D], F32, kind="ExternalInput").ap()
    wg_h = nc.dram_tensor("wg", [HID, D], I32, kind="ExternalInput").ap()
    wu_h = nc.dram_tensor("wu", [HID, D], I32, kind="ExternalInput").ap()
    wd_h = nc.dram_tensor("wd", [D, HID], I32, kind="ExternalInput").ap()
    tabC_h = nc.dram_tensor("tabC", [T, D], BF16, kind="ExternalInput").ap()
    tabS_h = nc.dram_tensor("tabS", [T, D], BF16, kind="ExternalInput").ap()
    masks_h = nc.dram_tensor("masks", [4, 128, 512], BF16, kind="ExternalInput").ap()
    identb_h = nc.dram_tensor("identb_h", [128, 128], BF16, kind="ExternalInput").ap()
    identf_h = nc.dram_tensor("identf_h", [128, 128], F32, kind="ExternalInput").ap()
    onesb_h = nc.dram_tensor("onesb_h", [1, 64], BF16, kind="ExternalInput").ap()
    onesf_h = nc.dram_tensor("onesf_h", [1, 128], F32, kind="ExternalInput").ap()
    out_h = nc.dram_tensor("out", [T, D], F32, kind="ExternalOutput").ap()

    xn_kind = "ExternalOutput"
    o_kind = "ExternalOutput"
    xn_sp = nc.dram_tensor("xn_sp", [NCH, 128, 8, 128], BF16, kind=xn_kind).ap()
    o_sp = nc.dram_tensor("o_sp", [NCH, 128, 8, 128], BF16, kind=o_kind).ap()
    if "kv" in dbg:
        k_dbg = nc.dram_tensor("k_dbg", [128, 8, T], BF16, kind="ExternalOutput").ap()
        v_dbg = nc.dram_tensor("v_dbg", [128, NCH, 1040], BF16, kind="ExternalOutput").ap()
    if "wt" in dbg:
        wkT_dbg = nc.dram_tensor("wkT_dbg", [128, 8, D], BF16, kind="ExternalOutput").ap()
        wvT_dbg = nc.dram_tensor("wvT_dbg", [128, 8, D], BF16, kind="ExternalOutput").ap()

    with contextlib.ExitStack() as st:
        sems = {n: st.enter_context(nc.semaphore(n)) for n in SEM_NAMES}
        p = Plan(sems)

        identb = st.enter_context(nc.sbuf_tensor("identb", [128, 128], BF16))
        identf = st.enter_context(nc.sbuf_tensor("identf", [128, 128], F32))
        onesb = st.enter_context(nc.sbuf_tensor("onesb", [1, 64], BF16))
        onesf = st.enter_context(nc.sbuf_tensor("onesf", [1, 128], F32))
        epscol = st.enter_context(nc.sbuf_tensor("epscol", [128, 1], F32))
        Mkcol = st.enter_context(nc.sbuf_tensor("Mkcol", [16, 1], F32))

        def prep_weight(w_src, rows, wT, nblk, stage, pwps):
            seg = stage.shape[1]
            for s0 in range(0, rows, seg):
                ns = min(seg, rows - s0)
                p.op("gpsimd",
                     lambda g, s0=s0, ns=ns: g.dma_start(out=stage[:, 0:ns, :], in_=w_src[:, s0:s0 + ns, :]),
                     waits=[("pe", p.tok("pe")), ("lw", p.tok("lw"))], incs=[("lw", 16)])
                lwtok = p.tok("lw")
                for blk in range(nblk):
                    for gl in range(0, ns, 8):
                        ng = min(8, ns - gl)
                        for al in range(gl, gl + ng):
                            p.op("tensor",
                                 lambda t_, al=al, blk=blk, gl=gl: t_.transpose(
                                     pwps[:, (al - gl) * 128:(al - gl + 1) * 128],
                                     stage[:, al, blk * 128:(blk + 1) * 128], identb[:]),
                                 waits=([("lw", lwtok), ("dve", p.tok("dve"))] if al == gl else []),
                                 incs=([("pe", 1)] if al == gl + ng - 1 else []))
                        p.op("vector",
                             lambda v, blk=blk, s0=s0, gl=gl, ng=ng: v.tensor_copy(
                                 wT[:, blk, (s0 + gl) * 128:(s0 + gl + ng) * 128],
                                 pwps[:, 0:ng * 128]),
                             waits=[("pe", p.tok("pe"))], incs=[("dve", 1)])

        with contextlib.ExitStack() as s12:
            K_res = s12.enter_context(nc.sbuf_tensor("K_res", [128, 8, T], BF16))
            V_res = s12.enter_context(nc.sbuf_tensor("V_res", [128, NCH, 1040], BF16))

            with contextlib.ExitStack() as p1:
                wkT = p1.enter_context(nc.sbuf_tensor("wkT", [128, 8, D], BF16))
                wvT = p1.enter_context(nc.sbuf_tensor("wvT", [128, 8, D], BF16))
                xbuf = [p1.enter_context(nc.sbuf_tensor(f"xbuf{i}", [128, D], F32)) for i in range(2)]
                tC = [p1.enter_context(nc.sbuf_tensor(f"tC{i}", [128, D], BF16)) for i in range(2)]
                tS = [p1.enter_context(nc.sbuf_tensor(f"tS{i}", [128, D], BF16)) for i in range(2)]
                xnb = [p1.enter_context(nc.sbuf_tensor(f"xnb{i}", [128, D], BF16)) for i in range(2)]
                xstg = [p1.enter_context(nc.sbuf_tensor(f"xstg{i}", [128, 8, 128], BF16)) for i in range(2)]
                krb = [p1.enter_context(nc.sbuf_tensor(f"krb{i}", [128, D], BF16)) for i in range(2)]
                t1b = p1.enter_context(nc.sbuf_tensor("t1b", [128, D], F32))
                t2b = p1.enter_context(nc.sbuf_tensor("t2b", [128, D], F32))
                khd = p1.enter_context(nc.sbuf_tensor("khd", [128, 16], F32))
                Mk2 = p1.enter_context(nc.sbuf_tensor("Mk2", [128, 16], F32))
                rcol = p1.enter_context(nc.sbuf_tensor("rcol", [128, 1], F32))
                mcol = p1.enter_context(nc.sbuf_tensor("mcol", [128, 1], F32))
                vcol = p1.enter_context(nc.sbuf_tensor("vcol", [128, 1], F32))
                rscol = p1.enter_context(nc.sbuf_tensor("rscol", [128, 1], F32))
                rstdc = p1.enter_context(nc.sbuf_tensor("rstdc", [128, 1], F32))
                mk2c = p1.enter_context(nc.sbuf_tensor("mk2c", [16, 1], F32))

                pk = p1.enter_context(nc.psum_tensor("pk", [128, D], F32))
                pv = p1.enter_context(nc.psum_tensor("pv", [128, D], F32))
                pxnT = p1.enter_context(nc.psum_tensor("pxnT", [128, D], BF16))
                pkT = p1.enter_context(nc.psum_tensor("pkT", [128, D], BF16))
                pw = p1.enter_context(nc.psum_tensor("pw", [128, D], BF16))

                p.op("gpsimd", lambda g: g.dma_start(out=identb[:], in_=identb_h), incs=[("lw", 16)])
                p.op("gpsimd", lambda g: g.dma_start(out=identf[:], in_=identf_h), incs=[("lw", 16)])
                p.op("gpsimd", lambda g: g.dma_start(out=onesb[:], in_=onesb_h), incs=[("lw", 16)])
                p.op("gpsimd", lambda g: g.dma_start(out=onesf[:], in_=onesf_h), incs=[("lw", 16)])
                p.op("vector", lambda v: v.memset(epscol[:], LN_EPS), incs=[("dve", 1)])
                p.op("vector", lambda v: v.memset(Mk2[:], 0.0), incs=[("dve", 1)])
                ones_view = V_res[:].rearrange("p c (h e) -> p c h e", e=65)[:, :, :, 64:65]
                p.op("vector", lambda v: v.memset(ones_view, 1.0), incs=[("dve", 1)])

                with nc.sbuf_tensor("wstg", [128, 4, D], BF16) as wstg:
                    prep_weight(wk_h.rearrange("(a p) d -> p a d", p=128), 8, wkT, 8, wstg, pw)
                    prep_weight(wv_h.rearrange("(a p) d -> p a d", p=128), 8, wvT, 8, wstg, pw)
                pe_prep = p.tok("pe")
                dve_prep = p.tok("dve")

                lx = [0, 0]
                lt = [0, 0]
                spx2 = [0, 0]
                xn_tok = 0
                xnT_toks = [0, 0]
                kT_toks = [0, 0]
                sqrt_tok = [0, 0]
                rope_tok = [0, 0]
                evx_toks = [0, 0]
                mv_tok = [0, 0]
                evv_tok = 0
                evk_tok = 0
                for c in range(NCH):
                    sl = c % 2
                    xw = [("dve", xn_tok), ("act", sqrt_tok[sl])]
                    if c < 2:
                        xw += [("pe", pe_prep), ("dve", dve_prep)]
                    p.op("gpsimd", lambda g, c=c, sl=sl: g.dma_start(out=xbuf[sl][:], in_=x_h[c * 128:(c + 1) * 128, :]),
                         waits=xw, incs=[(f"lx{sl}", 16)])
                    lx[sl] += 16
                    p.op("gpsimd", lambda g, c=c, sl=sl: g.dma_start(out=tC[sl][:], in_=tabC_h[c * 128:(c + 1) * 128, :]),
                         waits=[("dve", rope_tok[sl])], incs=[(f"lt{sl}", 16)])
                    p.op("gpsimd", lambda g, c=c, sl=sl: g.dma_start(out=tS[sl][:], in_=tabS_h[c * 128:(c + 1) * 128, :]),
                         incs=[(f"lt{sl}", 16)])
                    lt[sl] += 32
                    p.op("vector", lambda v, sl=sl: v.reduce_sum(rcol[:], xbuf[sl][:], axis=AX.X),
                         waits=[(f"lx{sl}", lx[sl])])
                    p.op("vector", lambda v: v.tensor_scalar_mul(mcol[:], rcol[:], INV_D))
                    tt = p.op("vector", lambda v, sl=sl: v.tensor_scalar_sub(xbuf[sl][:], xbuf[sl][:], mcol[:, 0:1]),
                              incs=[("dve", 1)])
                    xc_tok = tt["dve"]
                    p.op("scalar", lambda s, sl=sl: s.activation(t2b[:], xbuf[sl][:], AF.Square, accum_out=vcol[:, 0:1]),
                         waits=[("dve", xc_tok)])
                    tt = p.op("scalar", lambda s: s.activation(rscol[:], vcol[:, 0:1], AF.Sqrt, bias=epscol[:, 0:1], scale=INV_D),
                              incs=[("act", 1)])
                    sqrt_tok[sl] = tt["act"]
                    p.op("vector", lambda v: v.reciprocal(rstdc[:], rscol[:]), waits=[("act", sqrt_tok[sl])])
                    tt = p.op("vector", lambda v, sl=sl: v.tensor_scalar(xnb[sl][:], xbuf[sl][:], rstdc[:, 0:1], None, ALU.mult),
                              waits=[("pe", xnT_toks[sl])], incs=[("dve", 1)])
                    xn_tok = tt["dve"]
                    tt = p.op("tensor", lambda t_, sl=sl: [
                            t_.transpose(pxnT[:, j * 128:(j + 1) * 128], xnb[sl][:, j * 128:(j + 1) * 128], identb[:])
                            for j in range(8)][-1],
                            waits=[("dve", xn_tok), ("dve", evx_toks[1 - sl])], incs=[("pe", 1)])
                    xnT_toks[sl] = tt["pe"]
                    tt = p.op("vector", lambda v, sl=sl: v.tensor_copy(xstg[sl][:], pxnT[:].rearrange("p (j t) -> p j t", t=128)),
                              waits=[("pe", xnT_toks[sl]), ("pe", mv_tok[sl]), (f"spx{sl}", spx2[sl])], incs=[("dve", 1)])
                    evx_toks[sl] = tt["dve"]
                    p.op("sync", lambda s, c=c, sl=sl: s.dma_start(out=xn_sp[c], in_=xstg[sl][:]),
                         waits=[("dve", evx_toks[sl])], incs=[(f"spx{sl}", 16)])
                    spx2[sl] += 16
                    def kv_mms(t_, sl, wT, dst):
                        last = None
                        for osl in range(2):
                            for dc in range(8):
                                last = t_.matmul(dst[:, osl * 512:(osl + 1) * 512],
                                                 xstg[sl][:, dc, :],
                                                 wT[:, dc, osl * 512:(osl + 1) * 512],
                                                 start=(dc == 0), stop=(dc == 7))
                        return last
                    tt = p.op("tensor", lambda t_, sl=sl: kv_mms(t_, sl, wkT, pk),
                              waits=[("dve", evx_toks[sl]), ("dve", rope_tok[1 - sl])], incs=[("pe", 1)])
                    mk_tok = tt["pe"]
                    tt = p.op("tensor", lambda t_, sl=sl: kv_mms(t_, sl, wvT, pv),
                              waits=[("dve", evv_tok)], incs=[("pe", 1)])
                    mv_tok[sl] = tt["pe"]
                    p.op("vector", lambda v, sl=sl: v.tensor_tensor(t1b[:], pk[:], tC[sl][:], ALU.mult),
                         waits=[("pe", mk_tok), (f"lt{sl}", lt[sl]), ("act", sqrt_tok[sl])])
                    p.op("vector", lambda v, sl=sl: v.tensor_tensor(t2b[:, 0::2], pk[:, 1::2], tS[sl][:, 0::2], ALU.mult))
                    p.op("vector", lambda v, sl=sl: v.tensor_tensor(t2b[:, 1::2], pk[:, 0::2], tS[sl][:, 1::2], ALU.mult))
                    tt = p.op("vector", lambda v, sl=sl: v.tensor_add(krb[sl][:], t1b[:], t2b[:]),
                              waits=[("pe", kT_toks[sl])], incs=[("dve", 1)])
                    rope_tok[sl] = tt["dve"]
                    p.op("vector", lambda v, sl=sl: v.tensor_tensor(t1b[:], krb[sl][:], krb[sl][:], ALU.mult))
                    p.op("vector", lambda v: v.reduce_sum(khd[:].unsqueeze(2), t1b[:].rearrange("p (h e) -> p h e", e=64), axis=AX.X))
                    p.op("vector", lambda v: v.tensor_max(Mk2[:], Mk2[:], khd[:]))
                    vdst = V_res[:].rearrange("p c (h e) -> p c h e", e=65)
                    tt = p.op("vector", lambda v, c=c, vdst=vdst: v.tensor_copy(
                            vdst[:, c, :, 0:64], pv[:].rearrange("p (h e) -> p h e", e=64)),
                            waits=[("pe", mv_tok[sl])], incs=[("dve", 1)])
                    evv_tok = tt["dve"]
                    tt = p.op("tensor", lambda t_, sl=sl: [
                            t_.transpose(pkT[:, j * 128:(j + 1) * 128], krb[sl][:, j * 128:(j + 1) * 128], identb[:])
                            for j in range(8)][-1],
                            waits=[("dve", rope_tok[sl]), ("dve", evk_tok)], incs=[("pe", 1)])
                    kT_toks[sl] = tt["pe"]
                    tt = p.op("vector", lambda v, c=c: v.tensor_copy(
                            K_res[:][:, :, c * 128:(c + 1) * 128],
                            pkT[:].rearrange("p (j t) -> p j t", t=128)),
                            waits=[("pe", tt["pe"])], incs=[("dve", 1)])
                    evk_tok = tt["dve"]

                pmk = pw[:].bitcast(F32)[0:16, 0:128]
                tt = p.op("tensor", lambda t_: t_.transpose(pmk, Mk2[:, 0:16], identf[:]),
                          waits=[("dve", p.tok("dve"))], incs=[("pe", 1)])
                tt = p.op("vector", lambda v: v.reduce_max(mk2c[:], pmk, axis=AX.X),
                          waits=[("pe", tt["pe"])], incs=[("dve", 1)])
                p.op("vector", lambda v: v.tensor_copy(Mkcol[:], mk2c[:]), incs=[("dve", 1)])
                if "kv" in dbg:
                    p.op("sync", lambda s: s.dma_start(out=k_dbg, in_=K_res[:]),
                         waits=[("dve", p.tok("dve"))], incs=[("out0", 16)])
                    p.op("sync", lambda s: s.dma_start(out=v_dbg, in_=V_res[:]),
                         incs=[("out0", 16)])
                if "wt" in dbg:
                    p.op("sync", lambda s: s.dma_start(out=wkT_dbg, in_=wkT[:]),
                         waits=[("dve", p.tok("dve"))], incs=[("out0", 16)])
                    p.op("sync", lambda s: s.dma_start(out=wvT_dbg, in_=wvT[:]),
                         incs=[("out0", 16)])
                p.barrier()
                p.emit_phase(nc)

            with contextlib.ExitStack() as p2:
                wqT = p2.enter_context(nc.sbuf_tensor("wqT", [128, 8, D], BF16))
                maskt = p2.enter_context(nc.sbuf_tensor("maskt", [128, 4, 512], BF16))
                xnq = [p2.enter_context(nc.sbuf_tensor(f"xnq{i}", [128, 8, 128], BF16)) for i in range(2)]
                tC2 = [p2.enter_context(nc.sbuf_tensor(f"tC2{i}", [128, D], BF16)) for i in range(2)]
                tS2 = [p2.enter_context(nc.sbuf_tensor(f"tS2{i}", [128, D], BF16)) for i in range(2)]
                qrb = p2.enter_context(nc.sbuf_tensor("qrb", [128, D], BF16))
                Qfm = p2.enter_context(nc.sbuf_tensor("Qfm", [128, 8, 512], BF16))
                t1q = p2.enter_context(nc.sbuf_tensor("t1q", [128, D], F32))
                t2q = p2.enter_context(nc.sbuf_tensor("t2q", [128, D], F32))
                qhd = p2.enter_context(nc.sbuf_tensor("qhd", [128, 16], F32))
                Mq2 = p2.enter_context(nc.sbuf_tensor("Mq2", [128, 16], F32))
                mq2c = p2.enter_context(nc.sbuf_tensor("mq2c", [16, 1], F32))
                mqc = p2.enter_context(nc.sbuf_tensor("mqc", [16, 1], F32))
                b16 = p2.enter_context(nc.sbuf_tensor("b16", [16, 1], F32))
                bnr = p2.enter_context(nc.sbuf_tensor("bnr", [16, 1], I32))
                btb = p2.enter_context(nc.sbuf_tensor("btb", [16, 1], F32))
                brow = p2.enter_context(nc.sbuf_tensor("brow", [1, 16], F32))
                bpair = p2.enter_context(nc.sbuf_tensor("bpair", [1, 8], F32))
                bcols = p2.enter_context(nc.sbuf_tensor("bcols", [128, 8], F32))
                ptb = [p2.enter_context(nc.sbuf_tensor(f"ptb{i}", [128, 1024], BF16)) for i in range(3)]
                recL = [p2.enter_context(nc.sbuf_tensor(f"recL{i}", [1, 512], BF16)) for i in range(2)]
                onesr = p2.enter_context(nc.sbuf_tensor("onesr", [1, 512], F32))
                recLf = p2.enter_context(nc.sbuf_tensor("recLf", [1, 512], F32))
                Osb = p2.enter_context(nc.sbuf_tensor("Osb", [64, 512], F32))
                Ostg = p2.enter_context(nc.sbuf_tensor("Ostg", [128, 8, 512], BF16))

                pS = [p2.enter_context(nc.psum_tensor(f"pS{i}", [128, 1024], F32)) for i in range(2)]
                pO = [[p2.enter_context(nc.psum_tensor(f"pO{i}{hh}", [128, 512], F32)) for hh in range(2)] for i in range(2)]
                pq0, pq1 = pO[0][0], pO[0][1]

                p.op("gpsimd", lambda g: g.dma_start(out=maskt[:], in_=masks_h.rearrange("r p n -> p r n")),
                     incs=[("lw", 16)])
                p.op("vector", lambda v: v.memset(onesr[:], 1.0), incs=[("dve", 1)])
                pwq = pq0[:].bitcast(BF16)[:, 0:1024]
                with nc.sbuf_tensor("wstg2", [128, 2, D], BF16) as wstg2:
                    prep_weight(wq_h.rearrange("(a p) d -> p a d", p=128), 8, wqT, 8, wstg2, pwq)
                pe_prep2 = p.tok("pe")
                dve_prep2 = p.tok("dve")

                lxn = 0
                lq = [0, 0]
                spo = 0
                rope2_tok = [0, 0]
                qT_tok = 0
                evq_tok = 0
                mmq_last = 0
                oev_act = 0
                oev_dve = 0
                oev_set = [(0, 0), (0, 0)]
                oev_mult = 0
                lb_tok = 0
                for j in range(NQT):
                    for ts in range(4):
                        sl = ts % 2
                        cidx = 4 * j + ts
                        qsl = cidx % 2
                        if not (ts == 0 and j > 0):
                            xw = [("pe", mmq_last), ("spx0", 256), ("spx1", 256)]
                            if j == 0 and ts == 0:
                                xw += [("pe", pe_prep2), ("dve", dve_prep2)]
                            p.op("gpsimd", lambda g, cidx=cidx, qsl=qsl: g.dma_start(out=xnq[qsl][:], in_=xn_sp[cidx]),
                                 waits=xw, incs=[("lxn", 16)])
                            lxn += 16
                        if not (ts == 0 and j > 0):
                            p.op("gpsimd", lambda g, cidx=cidx, sl=sl: g.dma_start(out=tC2[sl][:], in_=tabC_h[cidx * 128:(cidx + 1) * 128, :]),
                                 waits=[("dve", rope2_tok[sl])], incs=[(f"lq{sl}", 16)])
                            p.op("gpsimd", lambda g, cidx=cidx, sl=sl: g.dma_start(out=tS2[sl][:], in_=tabS_h[cidx * 128:(cidx + 1) * 128, :]),
                                 incs=[(f"lq{sl}", 16)])
                            lq[sl] += 32
                        waits = [("lxn", lxn), ("dve", evq_tok), ("dve", rope2_tok[1 - sl])]
                        if ts == 0:
                            waits += [("dve", oev_dve)]
                        def q_mms(t_, qsl=qsl):
                            last = None
                            for osl, dst in ((0, pq0), (1, pq1)):
                                for dc in range(8):
                                    last = t_.matmul(dst[:],
                                                     xnq[qsl][:, dc, :],
                                                     wqT[:, dc, osl * 512:(osl + 1) * 512],
                                                     start=(dc == 0), stop=(dc == 7))
                            return last
                        tt = p.op("tensor", q_mms, waits=waits, incs=[("pe", 1)])
                        mmq_last = tt["pe"]
                        for half, psrc in ((0, pq0), (1, pq1)):
                            h0 = half * 512
                            p.op("vector", lambda v, sl=sl, psrc=psrc, h0=h0: v.tensor_tensor(
                                    t1q[:, h0:h0 + 512], psrc[:], tC2[sl][:, h0:h0 + 512], ALU.mult),
                                 waits=([("pe", mmq_last), (f"lq{sl}", lq[sl])] if half == 0 else []))
                            p.op("vector", lambda v, sl=sl, psrc=psrc, h0=h0: v.tensor_tensor(
                                    t2q[:, h0:h0 + 512][:, 0::2], psrc[:, 1::2], tS2[sl][:, h0:h0 + 512][:, 0::2], ALU.mult))
                            p.op("vector", lambda v, sl=sl, psrc=psrc, h0=h0: v.tensor_tensor(
                                    t2q[:, h0:h0 + 512][:, 1::2], psrc[:, 0::2], tS2[sl][:, h0:h0 + 512][:, 1::2], ALU.mult))
                        tt = p.op("vector", lambda v: v.tensor_add(qrb[:], t1q[:], t2q[:]),
                                  waits=[("pe", qT_tok)], incs=[("dve", 1)])
                        rope2_tok[sl] = tt["dve"]
                        p.op("vector", lambda v: v.tensor_tensor(t1q[:], qrb[:], qrb[:], ALU.mult))
                        p.op("vector", lambda v: v.reduce_sum(qhd[:].unsqueeze(2), t1q[:].rearrange("p (h e) -> p h e", e=64), axis=AX.X))
                        if ts == 0:
                            p.op("vector", lambda v: v.tensor_copy(Mq2[:], qhd[:]))
                        else:
                            p.op("vector", lambda v: v.tensor_max(Mq2[:], Mq2[:], qhd[:]))
                        tt = p.op("tensor", lambda t_: [
                                t_.transpose(pwq[:, jj * 128:(jj + 1) * 128], qrb[:, jj * 128:(jj + 1) * 128], identb[:])
                                for jj in range(8)][-1],
                                waits=[("dve", rope2_tok[sl])], incs=[("pe", 1)])
                        qT_tok = tt["pe"]
                        tt = p.op("vector", lambda v, ts=ts: v.tensor_copy(
                                Qfm[:, :, ts * 128:(ts + 1) * 128],
                                pwq[:].rearrange("p (jj t) -> p jj t", t=128)),
                                waits=[("pe", tt["pe"])], incs=[("dve", 1)])
                        evq_tok = tt["dve"]

                    tt = p.op("tensor", lambda t_: t_.transpose(pq0[:].bitcast(F32)[0:16, 0:128], Mq2[:, 0:16], identf[:]),
                              waits=[("dve", p.tok("dve"))], incs=[("pe", 1)])
                    tt = p.op("vector", lambda v: v.reduce_max(mq2c[:], pq0[0:16, 0:128], axis=AX.X),
                              waits=[("pe", tt["pe"])], incs=[("dve", 1)])
                    p.op("vector", lambda v: v.tensor_mul(b16[:], mq2c[:], Mkcol[:]))
                    ynr = bnr[:].bitcast(F32)
                    p.op("vector", lambda v: v.tensor_scalar(bnr[:], b16[:].bitcast(I32), 1, None, ALU.arith_shift_right))
                    p.op("vector", lambda v: v.tensor_scalar(bnr[:], bnr[:], -1, 0x5F3759DF, ALU.mult, ALU.add))
                    for _nr in range(2):
                        p.op("vector", lambda v: v.tensor_mul(btb[:], b16[:], ynr))
                        p.op("vector", lambda v: v.tensor_mul(btb[:], btb[:], ynr))
                        p.op("vector", lambda v: v.tensor_scalar(btb[:], btb[:], -0.5, 1.5, ALU.mult, ALU.add))
                        p.op("vector", lambda v: v.tensor_mul(ynr, ynr, btb[:]))
                    p.op("vector", lambda v: v.tensor_mul(b16[:], b16[:], ynr))
                    tt = p.op("vector", lambda v: v.tensor_scalar(b16[:], b16[:], -1.01, -0.5, ALU.mult, ALU.add),
                              incs=[("dve", 1)])
                    tt = p.op("tensor", lambda t_: t_.transpose(pq0[:].bitcast(F32)[0:1, 0:16], b16[:], identf[0:16, 0:16]),
                              waits=[("dve", tt["dve"])], incs=[("pe", 1)])
                    p.op("vector", lambda v: v.tensor_copy(brow[:], pq0[0:1, 0:16]),
                         waits=[("pe", tt["pe"])])
                    tt = p.op("vector", lambda v: v.tensor_tensor(bpair[:], brow[:, 0::2], brow[:, 1::2], ALU.min),
                              incs=[("dve", 1)])
                    tt = p.op("tensor", lambda t_: t_.matmul(pq0[:, 0:8], onesf[:], bpair[:], start=True, stop=True),
                              waits=[("dve", tt["dve"])], incs=[("pe", 1)])
                    tt = p.op("vector", lambda v: v.tensor_copy(bcols[:], pq0[:, 0:8]),
                              waits=[("pe", tt["pe"])], incs=[("dve", 1)])
                    bc_tok = tt["dve"]

                    nch_j = 4 * (j + 1)
                    insts = [(pr, c) for pr in range(8) for c in range(nch_j)]
                    exp_toks = {}
                    av_toks = {}
                    mask_toks = {}
                    evac_jobs = {}

                    def emit_s_exp(i):
                        pr, c = insts[i]
                        sl2 = i % 2
                        sl3 = i % 3
                        dr = c - 4 * j
                        def s_mms(t_, pr=pr, c=c, sl2=sl2):
                            t_.matmul(pS[sl2][:, 0:512], K_res[0:64, pr, c * 128:(c + 1) * 128],
                                      Qfm[0:64, pr, :], start=True, stop=True)
                            return t_.matmul(pS[sl2][:, 512:1024], K_res[64:128, pr, c * 128:(c + 1) * 128],
                                             Qfm[64:128, pr, :], start=True, stop=True,
                                             tile_position=(64, 0))
                        tt = p.op("tensor", s_mms,
                                  waits=[("act", exp_toks.get(i - 2, 0)), ("dve", bc_tok)],
                                  incs=[("pe", 1)])
                        tt = p.op("scalar", lambda s, pr=pr, sl3=sl3, sl2=sl2: s.activation(
                                ptb[sl3][:], pS[sl2][:], AF.Exp, bias=bcols[:, pr:pr + 1]),
                                waits=[("pe", tt["pe"]), ("pe", av_toks.get(i - 3, 0))],
                                incs=[("act", 1)])
                        exp_toks[i] = tt["act"]
                        if dr >= 0:
                            p.op("vector", lambda v, dr=dr, sl3=sl3: v.tensor_tensor(
                                    ptb[sl3][:, 0:512], ptb[sl3][:, 0:512], maskt[:, dr, :], ALU.mult),
                                    waits=[("act", exp_toks[i])])
                            tt = p.op("vector", lambda v, dr=dr, sl3=sl3: v.tensor_tensor(
                                    ptb[sl3][:, 512:1024], ptb[sl3][:, 512:1024], maskt[:, dr, :], ALU.mult),
                                    incs=[("dve", 1)])
                            mask_toks[i] = tt["dve"]

                    def emit_av(i):
                        nonlocal lb_tok
                        pr, c = insts[i]
                        sl2 = i % 2
                        st2 = pr % 2
                        waits = [("act", exp_toks[i])]
                        if i in mask_toks:
                            waits.append(("dve", mask_toks[i]))
                        if c == 0:
                            waits += [("dve", oev_set[st2][0])]
                        sl3 = i % 3
                        def av_mms(t_, pr=pr, c=c, sl3=sl3, st2=st2, first=(c == 0), last=(c == nch_j - 1)):
                            t_.matmul(pO[st2][0][0:65, :], V_res[:, c, 130 * pr:130 * pr + 65],
                                      ptb[sl3][:, 0:512], start=first, stop=last)
                            return t_.matmul(pO[st2][1][0:65, :], V_res[:, c, 130 * pr + 65:130 * pr + 130],
                                             ptb[sl3][:, 512:1024], start=first, stop=last)
                        tt = p.op("tensor", av_mms, waits=waits, incs=[("pe", 1)])
                        av_toks[i] = tt["pe"]
                        if c == nch_j - 1:
                            rec = emit_evac_early(pr, av_toks[i])
                            pending.append((i, pr, av_toks[i], rec))

                    def emit_evac_early(pr, av_last):
                        nonlocal lb_tok
                        rec = []
                        for hh in range(2):
                            bank = pO[pr % 2][hh]
                            p.op("vector", lambda v, bank=bank: v.reciprocal(recLf[:], bank[64:65, :]),
                                 waits=[("pe", av_last)])
                            tt = p.op("vector", lambda v, hh=hh: v.tensor_copy(recL[hh][:], recLf[:]),
                                      waits=[("pe", lb_tok)], incs=[("dve", 1)])
                            rec.append(tt["dve"])
                        return rec

                    def emit_evac_late(pr, av_last, rec):
                        nonlocal lb_tok, oev_mult
                        st2 = pr % 2
                        dv = 0
                        for hh in range(2):
                            bank = pO[st2][hh]
                            tt = p.op("vector", lambda v, bank=bank: v.tensor_copy(Osb[:], bank[0:64, :]),
                                      waits=[("dve", oev_mult)], incs=[("dve", 1)])
                            ac = tt["dve"]
                            tt = p.op("tensor", lambda t_, bank=bank, hh=hh: t_.matmul(bank[0:64, :], onesb[:], recL[hh][:], start=True, stop=True),
                                      waits=[("dve", max(rec[hh], ac))], incs=[("pe", 1)])
                            lb_tok = tt["pe"]
                            tt = p.op("vector", lambda v, hh=hh, pr=pr, bank=bank: v.tensor_tensor(
                                    Ostg[hh * 64:(hh + 1) * 64, pr, :], Osb[:], bank[0:64, :], ALU.mult),
                                    waits=[("pe", lb_tok), ("spo0", spo)],
                                    incs=[("dve", 1)])
                            dv = tt["dve"]
                            oev_mult = dv
                        oev_set[st2] = (dv, dv)

                    pending = []
                    n_inst = len(insts)
                    emit_s_exp(0)
                    if n_inst > 1:
                        emit_s_exp(1)
                    pf_i = max(2, (6 * n_inst) // 10)
                    for i in range(2, n_inst):
                        emit_s_exp(i)
                        while pending and pending[0][0] <= i - 3:
                            job = pending.pop(0)
                            emit_evac_late(job[1], job[2], job[3])
                        emit_av(i - 2)
                        if i == pf_i and j + 1 < NQT:
                            cnext = 4 * (j + 1)
                            p.op("sync", lambda g, cnext=cnext: g.dma_start(out=xnq[cnext % 2][:], in_=xn_sp[cnext]),
                                 waits=[("pe", mmq_last)], incs=[("lxn", 16)])
                            lxn += 16
                            p.op("gpsimd", lambda g, cnext=cnext: g.dma_start(out=tC2[0][:], in_=tabC_h[cnext * 128:(cnext + 1) * 128, :]),
                                 waits=[("dve", rope2_tok[0])], incs=[("lq0", 16)])
                            p.op("gpsimd", lambda g, cnext=cnext: g.dma_start(out=tS2[0][:], in_=tabS_h[cnext * 128:(cnext + 1) * 128, :]),
                                 incs=[("lq0", 16)])
                            lq[0] += 32
                    if n_inst > 1:
                        emit_av(n_inst - 2)
                    emit_av(n_inst - 1)
                    for job in pending:
                        emit_evac_late(job[1], job[2], job[3])
                    pending = []
                    pending = []
                    oev_dve = max(oev_set[0][0], oev_set[1][0])
                    oev_act = oev_dve
                    for cc in range(4):
                        p.op("sync", lambda s, j=j, cc=cc: s.dma_start(
                                out=o_sp[4 * j + cc], in_=Ostg[:, :, cc * 128:(cc + 1) * 128]),
                             waits=[("dve", oev_dve)], incs=[("spo0", 16)])
                    spo += 64
                p.barrier()
                p.emit_phase(nc)

        with contextlib.ExitStack() as p3:
            wgT = p3.enter_context(nc.sbuf_tensor("wgT", [128, 8, HID], BF16))
            wuT = p3.enter_context(nc.sbuf_tensor("wuT", [128, 8, HID], BF16))
            woT = p3.enter_context(nc.sbuf_tensor("woT", [128, 8, D], BF16))
            wdT = p3.enter_context(nc.sbuf_tensor("wdT", [128, 24, D], BF16))

            with nc.sbuf_tensor("wstg3", [128, 16, D], BF16) as wstg3, \
                 nc.psum_tensor("pw3", [128, D], BF16) as pw3:
                prep_weight(wg_h.rearrange("(a p) d -> p a d", p=128), 24, wgT, 8, wstg3, pw3)
                prep_weight(wu_h.rearrange("(a p) d -> p a d", p=128), 24, wuT, 8, wstg3, pw3)
                prep_weight(wo_h.rearrange("(a p) d -> p a d", p=128), 8, woT, 8, wstg3, pw3)
            with nc.sbuf_tensor("wstg4", [128, 2, HID], BF16) as wstg4, \
                 nc.psum_tensor("pw4", [128, D], BF16) as pw4:
                srcq = wd_h.rearrange("(a p) h -> p a h", p=128)
                for q in range(4):
                    p.op("gpsimd", lambda g, q=q: g.dma_start(out=wstg4[:], in_=srcq[:, 2 * q:2 * q + 2, :]),
                         waits=[("pe", p.tok("pe")), ("lw", p.tok("lw"))], incs=[("lw", 16)])
                    lwtok = p.tok("lw")
                    for a in range(2):
                        dc = 2 * q + a
                        for b in range(3):
                            for hc in range(8):
                                hcg = b * 8 + hc
                                p.op("tensor", lambda t_, a=a, hcg=hcg, hc=hc: t_.transpose(
                                        pw4[:, hc * 128:(hc + 1) * 128],
                                        wstg4[:, a, hcg * 128:(hcg + 1) * 128], identb[:]),
                                     waits=([("lw", lwtok), ("dve", p.tok("dve"))] if hc == 0 else []),
                                     incs=([("pe", 1)] if hc == 7 else []))
                            wdst = wdT[:][:, b * 8:(b + 1) * 8, dc * 128:(dc + 1) * 128]
                            p.op("vector", lambda v, wdst=wdst: v.tensor_copy(
                                    wdst, pw4[:].rearrange("p (hh t) -> p hh t", t=128)),
                                 waits=[("pe", p.tok("pe"))], incs=[("dve", 1)])
            pe_prep3 = p.tok("pe")
            dve_prep3 = p.tok("dve")

            xn3 = [p3.enter_context(nc.sbuf_tensor(f"xn3{i}", [128, 8, 128], BF16)) for i in range(2)]
            O3 = [p3.enter_context(nc.sbuf_tensor(f"O3{i}", [128, 8, 128], BF16)) for i in range(2)]
            x3 = p3.enter_context(nc.sbuf_tensor("x3", [128, D], F32))
            htm = p3.enter_context(nc.sbuf_tensor("htm", [128, HID], BF16))
            hfm = p3.enter_context(nc.sbuf_tensor("hfm", [128, 24, 128], BF16))
            gsil = p3.enter_context(nc.sbuf_tensor("gsil", [128, 512], F32))
            gdn = p3.enter_context(nc.sbuf_tensor("gdn", [128, 512], F32))
            sgf = p3.enter_context(nc.sbuf_tensor("sgf", [128, 512], F32))
            outb = [p3.enter_context(nc.sbuf_tensor(f"outb{i}", [128, D], F32)) for i in range(2)]

            pA = p3.enter_context(nc.psum_tensor("pA", [128, 512], F32))
            pB = p3.enter_context(nc.psum_tensor("pB", [128, 512], F32))
            pC = p3.enter_context(nc.psum_tensor("pC", [128, 512], F32))
            pD = p3.enter_context(nc.psum_tensor("pD", [128, 512], F32))
            phT = p3.enter_context(nc.psum_tensor("phT", [128, 1024], BF16))
            pd0 = p3.enter_context(nc.psum_tensor("pd0", [128, 512], F32))
            pd1 = p3.enter_context(nc.psum_tensor("pd1", [128, 512], F32))
            psj0 = p3.enter_context(nc.psum_tensor("psj0", [128, 512], F32))

            l3a = [0, 0]
            l3b = [0, 0]
            mmgu_tok = {}
            silu_tok = {}
            hmul_tok = {}
            sg_tok = {}
            ep_tok = [0, 0]
            sj_tok = [0, 0]
            dn_tok = [0, 0]
            he_tok = 0
            outc = [0, 0]
            for ti in range(NCH):
                sl = ti % 2
                xw = [("pe", mmgu_tok.get((ti - 2, 5), 0))]
                ow = [("pe", sj_tok[sl]), ("spo0", 512)]
                if ti < 2:
                    xw += [("pe", pe_prep3), ("dve", dve_prep3)]
                    ow += [("pe", pe_prep3), ("dve", dve_prep3)]
                p.op("gpsimd", lambda g, ti=ti, sl=sl: g.dma_start(out=xn3[sl][:], in_=xn_sp[ti]),
                     waits=xw, incs=[(f"l3a{sl}", 16)])
                p.op("gpsimd", lambda g, ti=ti, sl=sl: g.dma_start(out=O3[sl][:], in_=o_sp[ti]),
                     waits=ow, incs=[(f"l3a{sl}", 16)])
                l3a[sl] += 32
                xw2 = [("dve", sg_tok.get((ti - 1, 1), 0))]
                if ti < 2:
                    xw2 += [("dve", dve_prep3)]
                p.op("sync", lambda s, ti=ti: s.dma_start(out=x3[:], in_=x_h[ti * 128:(ti + 1) * 128, :]),
                     waits=xw2, incs=[("l3b0", 16)])
                l3b[sl] += 16
                hT_sched = {3: 0, 4: 1, 5: 2}
                for hs in range(6):
                    pg = (pA, pB)[hs % 2]
                    pu = (pC, pD)[hs % 2]
                    if hs >= 2:
                        frees = [("act", silu_tok.get((ti, hs - 2), 0)), ("dve", hmul_tok.get((ti, hs - 2), 0))]
                    else:
                        frees = [("act", silu_tok.get((ti - 1, hs + 4), 0)),
                                 ("dve", hmul_tok.get((ti - 1, hs + 4), 0))]
                    def gu_mms(t_, sl=sl, hs=hs, pg=pg, pu=pu):
                        last = None
                        for wT, dst in ((wgT, pg), (wuT, pu)):
                            for dc in range(8):
                                last = t_.matmul(dst[:], xn3[sl][:, dc, :],
                                                 wT[:, dc, hs * 512:(hs + 1) * 512],
                                                 start=(dc == 0), stop=(dc == 7))
                        return last
                    tt = p.op("tensor", gu_mms, waits=[(f"l3a{sl}", l3a[sl])] + frees, incs=[("pe", 1)])
                    mmgu_tok[(ti, hs)] = tt["pe"]
                    tt = p.op("scalar", lambda s, pg=pg: s.activation(gsil[:], pg[:], AF.Silu, scale=FFN_S),
                              waits=[("pe", tt["pe"]),
                                     ("dve", hmul_tok.get((ti, hs - 1), 0) if hs >= 1 else hmul_tok.get((ti - 1, 5), 0))],
                              incs=[("act", 1)])
                    silu_tok[(ti, hs)] = tt["act"]
                    tt = p.op("vector", lambda v, hs=hs, pu=pu: v.tensor_tensor(
                            htm[:, hs * 512:(hs + 1) * 512], gsil[:], pu[:], ALU.mult),
                            waits=[("act", tt["act"])], incs=[("dve", 1)])
                    hmul_tok[(ti, hs)] = tt["dve"]
                    if hs == 1:
                        def sj0_mms(t_, sl=sl):
                            last = None
                            for oc in range(8):
                                last = t_.matmul(psj0[:], O3[sl][:, oc, :], woT[:, oc, 0:512],
                                                 start=(oc == 0), stop=(oc == 7))
                            return last
                        tt = p.op("tensor", sj0_mms,
                                  waits=[(f"l3a{sl}", l3a[sl]), ("dve", sg_tok.get((ti - 1, 0), 0))],
                                  incs=[("pe", 1)])
                        sj0_tok = tt["pe"]
                for b in range(3):
                    tw = [("dve", hmul_tok[(ti, 2 * b + 1)]), ("dve", he_tok)]
                    if b == 0:
                        tw.append(("dve", sg_tok.get((ti - 1, 1), 0)))
                    tt = p.op("tensor", lambda t_, b=b: [
                            t_.transpose(phT[:, k * 128:(k + 1) * 128],
                                         htm[:, (b * 8 + k) * 128:(b * 8 + k + 1) * 128], identb[:])
                            for k in range(8)][-1],
                            waits=tw,
                            incs=[("pe", 1)])
                    tt = p.op("vector", lambda v, b=b: v.tensor_copy(
                            hfm[:, b * 8:(b + 1) * 8, :], phT[:].rearrange("p (k t) -> p k t", t=128)),
                            waits=[("pe", tt["pe"])], incs=[("dve", 1)])
                    he_tok = tt["dve"]
                for ds in range(2):
                    pd = (pd0, pd1)[ds]
                    psj = psj0 if ds == 0 else phT[:].bitcast(F32)[:, 0:512]
                    def dn_mms(t_, ds=ds, pd=pd):
                        last = None
                        for hc in range(24):
                            last = t_.matmul(pd[:], hfm[:, hc, :], wdT[:, hc, ds * 512:(ds + 1) * 512],
                                             start=(hc == 0), stop=(hc == 23))
                        return last
                    tt = p.op("tensor", dn_mms, waits=[("dve", he_tok), ("act", dn_tok[ds])], incs=[("pe", 1)])
                    pe_dn = tt["pe"]
                    def sj_mms(t_, ds=ds, psj=psj, sl=sl):
                        last = None
                        for oc in range(8):
                            last = t_.matmul(psj[:] if ds == 0 else psj, O3[sl][:, oc, :], woT[:, oc, ds * 512:(ds + 1) * 512],
                                             start=(oc == 0), stop=(oc == 7))
                        return last
                    if ds == 0:
                        sj_pe = sj0_tok
                    else:
                        tt = p.op("tensor", sj_mms, waits=[(f"l3a{sl}", l3a[sl]), ("dve", he_tok)],
                                  incs=[("pe", 1)])
                        sj_pe = tt["pe"]
                        sj_tok[sl] = sj_pe
                    gprev = sg_tok.get((ti, 0), 0) if ds == 1 else sg_tok.get((ti - 1, 1), 0)
                    tt = p.op("scalar", lambda s, pd=pd: s.activation(gdn[:], pd[:], AF.Silu, scale=FFN_S2),
                              waits=[("pe", pe_dn), ("dve", gprev)], incs=[("act", 1)])
                    dn_tok[ds] = tt["act"]
                    p.op("vector", lambda v, psj=psj: v.tensor_tensor(sgf[:], gdn[:], psj[:], ALU.mult),
                         waits=[("act", dn_tok[ds]), ("pe", sj_pe), ("l3b0", sum(l3b)),
                                (f"out{sl}", outc[sl])])
                    tt = p.op("vector", lambda v, ds=ds, sl=sl: v.tensor_add(
                            outb[sl][:, ds * 512:(ds + 1) * 512], sgf[:], x3[:, ds * 512:(ds + 1) * 512]),
                            incs=[("dve", 1)])
                    sg_tok[(ti, ds)] = tt["dve"]
                    ep_tok[sl] = tt["dve"]
                p.op("sync", lambda s, ti=ti, sl=sl: s.dma_start(out=out_h[ti * 128:(ti + 1) * 128, :], in_=outb[sl][:]),
                     waits=[("dve", ep_tok[sl])], incs=[(f"out{sl}", 16)])
                outc[sl] += 16
            p.op("sync", lambda s: s.nop(), waits=[("out0", p.tok("out0")), ("out1", p.tok("out1"))])
            p.emit_phase(nc)

    return nc


_NC_CACHE = {}


def _get_nc(dbg=()):
    key = tuple(sorted(dbg))
    if key not in _NC_CACHE:
        _NC_CACHE[key] = build_nc(dbg)
    return _NC_CACHE[key]


def kernel(L0_attn_x, L0_attn_wq, L0_attn_wk, L0_attn_wv, L0_attn_wo,
           L0_ffn_w_gate, L0_ffn_w_up, L0_ffn_w_down):
    consts = _host_consts()
    nc = _get_nc()
    x = np.ascontiguousarray(np.asarray(L0_attn_x, dtype=np.float32))
    shared = dict(
        wq=np.ascontiguousarray(np.asarray(L0_attn_wq, np.float32)),
        wk=np.ascontiguousarray(np.asarray(L0_attn_wk, np.float32)),
        wv=np.ascontiguousarray(np.asarray(L0_attn_wv, np.float32)),
        wo=np.ascontiguousarray(np.asarray(L0_attn_wo, np.float32)),
        wg=np.ascontiguousarray(np.asarray(L0_ffn_w_gate, np.int32)),
        wu=np.ascontiguousarray(np.asarray(L0_ffn_w_up, np.int32)),
        wd=np.ascontiguousarray(np.asarray(L0_ffn_w_down, np.int32)),
        tabC=consts["tabC"], tabS=consts["tabS"], masks=consts["masks"],
        identb_h=consts["identb"], identf_h=consts["identf"],
        onesb_h=consts["onesb"], onesf_h=consts["onesf"],
    )
    in_maps = [dict(x=x[b], **shared) for b in range(8)]
    res = run_bass_kernel_spmd(nc, in_maps, core_ids=list(range(8)))
    return np.stack([r["out"] for r in res.results], axis=0)
